# version 03 (final)
# baseline (speedup 1.0000x reference)
import math
import numpy as np
import ml_dtypes
import concourse.bass as bass
import concourse.mybir as mybir
from concourse.bass_utils import run_bass_kernel_spmd

F32 = mybir.dt.float32
BF16 = mybir.dt.bfloat16
U32 = mybir.dt.uint32
AF = mybir.ActivationFunctionType
ALU = mybir.AluOpType
AX = mybir.AxisListType
NPBF = ml_dtypes.bfloat16

D = 2048
EPS_LN = 1e-5
EPS_L0 = 1e-6
EPS_RMS = 1.1920929e-07

def tile_wT(w):
    O, Din = w.shape
    a = w.reshape(O // 128, 128, Din // 128, 128)
    return np.ascontiguousarray(a.transpose(2, 0, 3, 1))

def make_const_arrays():
    g64 = np.zeros((128, 2), dtype=NPBF)
    g64[0:64, 0] = 1; g64[64:128, 1] = 1
    sel64b = np.ascontiguousarray(g64.T)
    ones_col = np.ones((128, 1), dtype=NPBF)
    ones_row32 = np.ones((1, 128), dtype=np.float32)
    ident = np.eye(128, dtype=np.float32)
    identb = np.eye(128, dtype=NPBF)
    return {"G64": g64, "SEL64B": sel64b, "ONESC": ones_col,
            "ONESR32": ones_row32, "IDENT": ident, "IDENTB": identb}

CONST_DT = {"G64": BF16, "SEL64B": BF16, "ONESC": BF16, "ONESR32": F32,
            "IDENT": F32, "IDENTB": BF16}
CONST_SHAPE = {"G64": [128, 2], "SEL64B": [2, 128], "ONESC": [128, 1],
               "ONESR32": [1, 128], "IDENT": [128, 128], "IDENTB": [128, 128]}

class S:
    def __init__(self, stack, nc, name):
        self.h = stack.enter_context(nc.semaphore(name))
        self.n = 0
    def inc(self, instr, k=16):
        instr.then_inc(self.h, k)
        self.n += k
        return self.n
    def wait(self, eng, val=None):
        eng.wait_ge(self.h, self.n if val is None else val)

class SemRing:
    def __init__(self, ctx, name, R):
        self.R = R
        self.sems = [ctx.sem("%s%d" % (name, r)) for r in range(R)]
    def inc(self, instr, i, k=16):
        s = self.sems[i % self.R]
        s.n += k
        instr.then_inc(s.h, k)
    def wait(self, eng, i, per=16):
        s = self.sems[i % self.R]
        eng.wait_ge(s.h, per * (i // self.R + 1))
    def wait_all(self, eng):
        for s in self.sems:
            eng.wait_ge(s.h, s.n)

class Ctx:
    def __init__(self, nc, block, stack):
        self.nc = nc
        self.blk = block
        self.stack = stack
        self._sems = {}
        self._rings = {}
    def sem(self, name):
        if name not in self._sems:
            self._sems[name] = S(self.stack, self.nc, name)
        return self._sems[name]
    def ring(self, name, R):
        if name not in self._rings:
            self._rings[name] = SemRing(self, name, R)
        return self._rings[name]
    def sb(self, name, shape, dtype):
        return self.stack.enter_context(self.nc.sbuf_tensor(name, shape, dtype))
    def ps(self, name, shape, dtype=F32):
        return self.stack.enter_context(self.nc.psum_tensor(name, shape, dtype))
    def serial(self, eng, name):
        s = self.sem(name)
        def step(instr):
            s.n += 1
            instr.then_inc(s.h, 1)
            eng.wait_ge(s.h, s.n)
        return step

def phase_barrier(ctx, semnames=(), rings=()):
    s = ctx.sem("phb")
    base = s.n
    s.n = base + 5
    def f_sy(sy):
        for nm in semnames:
            ss = ctx.sem(nm)
            sy.wait_ge(ss.h, ss.n)
        for nm in rings:
            ctx._rings[nm].wait_all(sy)
        sy.drain().then_inc(s.h, 1)
        sy.wait_ge(s.h, base + 5)
    def mk(engname):
        def f(eng):
            eng.drain().then_inc(s.h, 1)
            eng.wait_ge(s.h, base + 5)
        return f
    ctx.blk.sync(f_sy)
    ctx.blk.tensor(mk("tensor"))
    ctx.blk.vector(mk("vector"))
    ctx.blk.scalar(mk("scalar"))
    ctx.blk.gpsimd(mk("gpsimd"))

def load_consts(ctx, dram_aps):
    out = {}
    s = ctx.sem("consts")
    def f(sy):
        for k, dt in CONST_DT.items():
            sb = ctx.sb("c_" + k, CONST_SHAPE[k], dt)
            out[k] = sb
            s.inc(sy.dma_start(out=sb[:], in_=dram_aps[k]))
    ctx.blk.sync(f)
    return out

def emit_quant(ctx, wt4, wq_out, ND, NO, C):
    nc, blk = ctx.nc, ctx.blk
    JW = min(16, NO)
    CW = JW * 128
    NB = 4
    units = [(k, j0) for k in range(ND) for j0 in range(0, NO, JW)]
    pre = "q_%s_" % wq_out.tensor.name
    rl = ctx.ring(pre + "ld", NB)
    rs = ctx.ring(pre + "st", NB)
    sa = ctx.sem(pre + "abs"); sm1 = ctx.sem(pre + "mm1")
    st_ = ctx.sem(pre + "thr"); sm2 = ctx.sem(pre + "mm2")
    stb = ctx.sem(pre + "tbc"); sw = ctx.sem(pre + "wq")
    with (
        nc.sbuf_tensor(pre + "w", [128, NB * CW], BF16) as wsb,
        nc.sbuf_tensor(pre + "a", [128, NB * CW], BF16) as asb,
        nc.sbuf_tensor(pre + "t2", [2, NB * CW], BF16) as t2sb,
        nc.sbuf_tensor(pre + "tb", [128, NB * CW], BF16) as tbsb,
        nc.sbuf_tensor(pre + "m", [128, NB * CW], BF16) as msb,
        nc.sbuf_tensor(pre + "n", [128, NB * CW], BF16) as nsb,
        nc.sbuf_tensor(pre + "t1", [128, NB * CW], BF16) as t1sb,
        nc.sbuf_tensor(pre + "o", [128, NB * CW], BF16) as osb,
        nc.psum_tensor(pre + "ps", [2, CW], F32) as ps_s,
        nc.psum_tensor(pre + "pb", [128, CW], F32) as ps_b,
    ):
        V = lambda sb, par: sb[:, par * CW:(par + 1) * CW]
        nch = CW // 512

        def f_gp(g):
            for i, (k, j0) in enumerate(units):
                par = i % NB
                if i >= NB:
                    rs.wait(g, i - NB)
                    g.wait_ge(sa.h, i - NB + 1)
                    g.wait_ge(sw.h, i - NB + 1)
                rl.inc(g.dma_start(
                    out=V(wsb, par).rearrange("d (j o) -> d j o", j=JW),
                    in_=wt4[k, j0:j0 + JW].rearrange("j d o -> d j o")), i)
        blk.gpsimd(f_gp)

        def f_te(t):
            t.wait_ge(ctx.sem("consts").h, ctx.sem("consts").n)
            for i, (k, j0) in enumerate(units):
                par = i % NB
                if i >= 1:
                    t.wait_ge(st_.h, i)
                t.wait_ge(sa.h, i + 1)
                for c in range(nch):
                    ins = t.matmul(ps_s[:, c*512:(c+1)*512], C["G64"][:],
                                   V(asb, par)[:, c*512:(c+1)*512],
                                   start=True, stop=True)
                ins.then_inc(sm1.h, 1)
                t.wait_ge(st_.h, i + 1)
                if i >= 1:
                    t.wait_ge(stb.h, i)
                for c in range(nch):
                    ins = t.matmul(ps_b[:, c*512:(c+1)*512], C["SEL64B"][:],
                                   V(t2sb, par)[:, c*512:(c+1)*512],
                                   start=True, stop=True)
                ins.then_inc(sm2.h, 1)
            sm1.n = sm2.n = len(units)
        blk.tensor(f_te)

        def f_ve(v):
            for i, (k, j0) in enumerate(units):
                par = i % NB
                v.wait_ge(sm1.h, i + 1)
                if i >= NB:
                    v.wait_ge(sm2.h, i - NB + 1)
                v.tensor_scalar(V(t2sb, par), ps_s[:, 0:CW], 1.0 / 64.0, 1e-8,
                                ALU.mult, ALU.max).then_inc(st_.h, 1)
                v.wait_ge(stb.h, i + 1)
                if i >= NB:
                    rs.wait(v, i - NB)
                step = ctx.serial(v, pre + "vs")
                step(v.scalar_tensor_tensor(V(msb, par), V(tbsb, par),
                                            0.501953125, V(wsb, par),
                                            ALU.mult, ALU.is_lt))
                step(v.scalar_tensor_tensor(V(nsb, par), V(tbsb, par),
                                            -0.501953125, V(wsb, par),
                                            ALU.mult, ALU.is_gt))
                step(v.tensor_sub(V(t1sb, par), V(msb, par), V(nsb, par)))
                v.tensor_mul(V(osb, par), V(t1sb, par),
                             V(tbsb, par)).then_inc(sw.h, 1)
            st_.n = sw.n = len(units)
        blk.vector(f_ve)

        def f_sc(sc):
            for i, (k, j0) in enumerate(units):
                par = i % NB
                rl.wait(sc, i)
                if i >= NB:
                    sc.wait_ge(sw.h, i - NB + 1)
                    sc.wait_ge(sm1.h, i - NB + 1)
                sc.activation(V(asb, par), V(wsb, par), AF.Abs).then_inc(sa.h, 1)
                sc.wait_ge(sm2.h, i + 1)
                sc.copy(V(tbsb, par), ps_b[:, 0:CW]).then_inc(stb.h, 1)
            stb.n = sa.n = len(units)
        blk.scalar(f_sc)

        def f_sy(sy):
            for i, (k, j0) in enumerate(units):
                par = i % NB
                sy.wait_ge(sw.h, i + 1)
                rs.inc(sy.dma_start(
                    out=wq_out[j0:j0 + JW, k].rearrange("j d o -> d j o"),
                    in_=V(osb, par).rearrange("d (j o) -> d j o", j=JW)), i)
        blk.sync(f_sy)

def emit_ln(ctx, x_dram, xn_dram, T, C, wab_sb=None, ab_dram=None, pre="ln"):
    nc, blk = ctx.nc, ctx.blk
    NT = T // 128
    rx = ctx.ring(pre + "x", 2)
    rxo = ctx.ring(pre + "xo", 2)
    rabo = ctx.ring(pre + "abo", 2)
    ssq = ctx.sem(pre + "sq"); sst = ctx.sem(pre + "st")
    ssd = ctx.sem(pre + "sd"); sxn = ctx.sem(pre + "xn")
    str_ = ctx.sem(pre + "tr"); stre = ctx.sem(pre + "tre")
    sab = ctx.sem(pre + "ab"); sabe = ctx.sem(pre + "abe")
    do_ab = wab_sb is not None
    with (
        nc.sbuf_tensor(pre + "xs", [128, 2 * D], F32) as xs,
        nc.sbuf_tensor(pre + "xq", [128, 2 * D], F32) as xq,
        nc.sbuf_tensor(pre + "stt", [128, 2 * 8], F32) as stt,
        nc.sbuf_tensor(pre + "xnf", [128, 2 * D], F32) as xnf,
        nc.sbuf_tensor(pre + "xnb", [128, 2 * D], BF16) as xnb,
        nc.sbuf_tensor(pre + "xt32", [128, 2 * D], F32) as xt32,
        nc.sbuf_tensor(pre + "abs_", [128, 2 * 64], F32) as absb,
        nc.psum_tensor(pre + "ptr", [128, 2048], F32) as ptr,
        nc.psum_tensor(pre + "pab", [128, 64], F32) as pab,
    ):
        V = lambda sb, par, w: sb[:, par * w:(par + 1) * w]
        def f_sy(sy):
            for i in range(NT):
                par = i % 2
                if i >= 2:
                    sy.wait_ge(sxn.h, i - 1)
                rx.inc(sy.dma_start(out=V(xs, par, D),
                                    in_=x_dram[128*i:128*(i+1), :]), i)
        blk.sync(f_sy)

        def f_sc(sc):
            for i in range(NT):
                par = i % 2
                rx.wait(sc, i)
                if i >= 2:
                    sc.wait_ge(sst.h, i - 1)
                sc.activation(V(xq, par, D), V(xs, par, D), AF.Square
                              ).then_inc(ssq.h, 1)
                sc.wait_ge(sst.h, i + 1)
                st = V(stt, par, 8)
                sc.activation(st[:, 6:7], st[:, 5:6], AF.Sqrt).then_inc(ssd.h, 1)
                if do_ab:
                    sc.wait_ge(str_.h, i + 1)
                    sc.copy(V(xt32, par, D), ptr[:, 0:2048]).then_inc(stre.h, 1)
            ssq.n = ssd.n = NT
            if do_ab:
                stre.n = NT
        blk.scalar(f_sc)

        def f_ve(v):
            for i in range(NT):
                par = i % 2
                st = V(stt, par, 8)
                step = ctx.serial(v, pre + "vs")
                rx.wait(v, i)
                step(v.reduce_sum(st[:, 0:1], V(xs, par, D), AX.X))
                v.wait_ge(ssq.h, i + 1)
                step(v.reduce_sum(st[:, 1:2], V(xq, par, D), AX.X))
                step(v.tensor_scalar(st[:, 2:3], st[:, 0:1], 1.0 / D, None,
                                     ALU.mult))
                step(v.tensor_scalar(st[:, 3:4], st[:, 1:2], 1.0 / D, None,
                                     ALU.mult))
                step(v.tensor_mul(st[:, 4:5], st[:, 2:3], st[:, 2:3]))
                v.scalar_tensor_tensor(st[:, 5:6], st[:, 3:4], EPS_LN,
                                       st[:, 4:5], ALU.add, ALU.subtract
                                       ).then_inc(sst.h, 1)
                v.wait_ge(ssd.h, i + 1)
                step(v.reciprocal(st[:, 7:8], st[:, 6:7]))
                if i >= 2 and do_ab:
                    v.wait_ge(str_.h, i - 1)
                step(v.tensor_scalar(V(xnf, par, D), V(xs, par, D), st[:, 2:3],
                                     st[:, 7:8], ALU.subtract, ALU.mult))
                if i >= 2:
                    rxo.wait(v, i - 2)
                v.tensor_copy(V(xnb, par, D), V(xnf, par, D)).then_inc(sxn.h, 1)
                if do_ab:
                    v.wait_ge(sab.h, i + 1)
                    if i >= 2:
                        rabo.wait(v, i - 2)
                    v.tensor_copy(V(absb, par, 64), pab[:]).then_inc(sabe.h, 1)
            sst.n = sxn.n = NT
            if do_ab:
                sabe.n = NT
        blk.vector(f_ve)

        if do_ab:
            def f_te(t):
                t.wait_ge(ctx.sem("consts").h, ctx.sem("consts").n)
                for i in range(NT):
                    par = i % 2
                    t.wait_ge(sxn.h, i + 1)
                    if i >= 1:
                        t.wait_ge(stre.h, i)
                        t.wait_ge(sabe.h, i)
                    ins = None
                    for k in range(16):
                        ins = t.transpose(ptr[:, 128*k:128*(k+1)],
                                          V(xnf, par, D)[:, 128*k:128*(k+1)],
                                          C["IDENT"][:])
                    ins.then_inc(str_.h, 1)
                    t.wait_ge(stre.h, i + 1)
                    for k in range(16):
                        ins = t.matmul(pab[:], V(xt32, par, D)[:, 128*k:128*(k+1)],
                                       wab_sb[:, 64*k:64*(k+1)],
                                       start=(k == 0), stop=(k == 15))
                    ins.then_inc(sab.h, 1)
                str_.n = sab.n = NT
            blk.tensor(f_te)

        def f_gp(g):
            for i in range(NT):
                par = i % 2
                g.wait_ge(sxn.h, i + 1)
                rxo.inc(g.dma_start(out=xn_dram[128*i:128*(i+1), :],
                                    in_=V(xnb, par, D)), i)
                if do_ab:
                    g.wait_ge(sabe.h, i + 1)
                    rabo.inc(g.dma_start(out=ab_dram[128*i:128*(i+1), :],
                                         in_=V(absb, par, 64)), i)
        blk.gpsimd(f_gp)

def emit_ffn0(ctx, xn_dram, x_in_dram, x_out_dram, wqgu_dram, wqp_dram,
              p_dram, T, C, pre="f0"):
    nc, blk = ctx.nc, ctx.blk
    TB = 1024
    NBK = T // TB
    NCH = TB // 512
    NTJ = TB // 128
    rxt = ctx.ring(pre + "xt", 2)
    rwl = ctx.ring(pre + "wl", 2)
    rwp = ctx.ring(pre + "wp", 2)
    rps = ctx.ring(pre + "ps", 2)
    rxl = ctx.ring(pre + "xl", 2)
    rpt = ctx.ring(pre + "pt", 2)
    rro = ctx.ring(pre + "ro", 2)
    shm = ctx.sem(pre + "hm"); ssl = ctx.sem(pre + "sl"); she = ctx.sem(pre + "he")
    shq = ctx.sem(pre + "hq"); ssm = ctx.sem(pre + "sm"); sin = ctx.sem(pre + "in")
    sib = ctx.sem(pre + "ib"); sie = ctx.sem(pre + "ie"); spm = ctx.sem(pre + "pm")
    spe = ctx.sem(pre + "pe"); sra = ctx.sem(pre + "ra")
    with (
        nc.sbuf_tensor(pre + "U", [128, 16384], BF16) as Usb,
        nc.sbuf_tensor(pre + "h", [128, 64 * TB], BF16) as hsb,
        nc.sbuf_tensor(pre + "wg", [128, 2 * 2048], BF16) as wgsb,
        nc.sbuf_tensor(pre + "wu", [128, 2 * 2048], BF16) as wusb,
        nc.sbuf_tensor(pre + "sq", [128, 2 * TB], BF16) as sqsb,
        nc.sbuf_tensor(pre + "sr", [1, TB], F32) as srow,
        nc.sbuf_tensor(pre + "ivb", [128, TB], F32) as ivbc,
        nc.sbuf_tensor(pre + "pb", [128, 2 * TB], BF16) as pbsb,
        nc.sbuf_tensor(pre + "xr", [128, D], F32) as xrsb,
    ):
        V = lambda sb, par, w: sb[:, par * w:(par + 1) * w]
        xnT = Usb
        wpv = lambda par: Usb[:, par * 8192:(par + 1) * 8192]
        for tb in range(NBK):
            c0 = tb * TB
            def f_a(sy, tb=tb, c0=c0):
                if tb >= 1:
                    sy.wait_ge(spm.h, tb * 16)
                for k in range(16):
                    rxt.inc(sy.dma_start(out=xnT[:, k*TB:(k+1)*TB],
                                         in_=xn_dram[c0:c0+TB, 128*k:128*(k+1)],
                                         transpose=True), tb)
            blk.sync(f_a)
            with (
                nc.psum_tensor(pre + "psg%d" % tb, [128, 2 * TB], F32) as psg,
                nc.psum_tensor(pre + "psu%d" % tb, [128, 2 * TB], F32) as psu,
            ):
                def f_b_sy(sy, tb=tb):
                    if tb >= 1:
                        sy.wait_ge(sra.h, tb * NTJ)
                    for pr in range(64):
                        i = tb * 64 + pr
                        par = pr % 2
                        if i >= 2:
                            sy.wait_ge(shm.h, i - 1)
                        rwl.inc(sy.dma_start(
                            out=V(wgsb, par, 2048).rearrange(
                                "d (k o) -> d k o", k=16),
                            in_=wqgu_dram[pr].rearrange("k d o -> d k o")), i)
                        rwl.inc(sy.dma_start(
                            out=V(wusb, par, 2048).rearrange(
                                "d (k o) -> d k o", k=16),
                            in_=wqgu_dram[64 + pr].rearrange("k d o -> d k o")), i)
                blk.sync(f_b_sy)
                def f_b_te(t, tb=tb, psg=psg, psu=psu):
                    rxt.wait(t, tb, per=256)
                    if tb >= 1:
                        t.wait_ge(spe.h, tb * 16)
                        t.wait_ge(sie.h, tb)
                    for pr in range(64):
                        i = tb * 64 + pr
                        par = pr % 2
                        rwl.wait(t, i, per=32)
                        if i >= 2:
                            t.wait_ge(she.h, i - 1)
                            t.wait_ge(ssl.h, i - 1)
                        for k in range(16):
                            for c in range(NCH):
                                t.matmul(psg[:, par*TB+c*512:par*TB+(c+1)*512],
                                         V(wgsb, par, 2048)[:, 128*k:128*(k+1)],
                                         xnT[:, k*TB+c*512:k*TB+(c+1)*512],
                                         start=(k == 0), stop=(k == 15))
                        ins = None
                        for k in range(16):
                            for c in range(NCH):
                                ins = t.matmul(
                                    psu[:, par*TB+c*512:par*TB+(c+1)*512],
                                    V(wusb, par, 2048)[:, 128*k:128*(k+1)],
                                    xnT[:, k*TB+c*512:k*TB+(c+1)*512],
                                    start=(k == 0), stop=(k == 15))
                        ins.then_inc(shm.h, 1)
                    shm.n += 64
                blk.tensor(f_b_te)
                def f_b_sc(sc, tb=tb, psg=psg):
                    for pr in range(64):
                        i = tb * 64 + pr
                        par = pr % 2
                        sc.wait_ge(shm.h, i + 1)
                        if i >= 2:
                            sc.wait_ge(she.h, i - 1)
                        sc.activation(V(sqsb, par, TB),
                                      psg[:, par*TB:(par+1)*TB],
                                      AF.Silu).then_inc(ssl.h, 1)
                    ssl.n += 64
                blk.scalar(f_b_sc)
                def f_b_ve(v, tb=tb, psu=psu):
                    for pr in range(64):
                        i = tb * 64 + pr
                        par = pr % 2
                        v.wait_ge(ssl.h, i + 1)
                        v.tensor_mul(hsb[:, pr*TB:(pr+1)*TB], V(sqsb, par, TB),
                                     psu[:, par*TB:(par+1)*TB]).then_inc(she.h, 1)
                    she.n += 64
                blk.vector(f_b_ve)
            with nc.psum_tensor(pre + "pss%d" % tb, [1, TB], F32) as pss:
                def f_b2_ve(v, tb=tb):
                    for kh in range(64):
                        i = tb * 64 + kh
                        par = kh % 2
                        if kh == 0:
                            v.wait_ge(she.h, (tb + 1) * 64)
                        if i >= 2:
                            v.wait_ge(ssm.h, i - 1)
                        v.tensor_mul(V(sqsb, par, TB), hsb[:, kh*TB:(kh+1)*TB],
                                     hsb[:, kh*TB:(kh+1)*TB]).then_inc(shq.h, 1)
                    shq.n += 64
                blk.vector(f_b2_ve)
                def f_b2_te(t, tb=tb, pss=pss):
                    t.wait_ge(ssl.h, (tb + 1) * 64)
                    t.wait_ge(she.h, (tb + 1) * 64)
                    for kh in range(64):
                        i = tb * 64 + kh
                        par = kh % 2
                        t.wait_ge(shq.h, i + 1)
                        ins = None
                        for c in range(TB // 512):
                            ins = t.matmul(pss[:, c*512:(c+1)*512], C["ONESC"][:],
                                           V(sqsb, par, TB)[:, c*512:(c+1)*512],
                                           start=(kh == 0), stop=(kh == 63))
                        ins.then_inc(ssm.h, 1)
                    ssm.n += 64
                blk.tensor(f_b2_te)
                def f_d_ve(v, tb=tb, pss=pss):
                    v.wait_ge(ssm.h, (tb + 1) * 64)
                    v.tensor_scalar(srow[0:1, :], pss[0:1, :], 1.0 / 8192.0,
                                    EPS_RMS, ALU.mult, ALU.add).then_inc(sin.h, 1)
                blk.vector(f_d_ve)
            def f_d_sc(sc, tb=tb):
                sc.wait_ge(sin.h, tb * 3 + 1)
                sc.activation(srow[0:1, :], srow[0:1, :], AF.Sqrt
                              ).then_inc(sin.h, 1)
            blk.scalar(f_d_sc)
            def f_d_ve2(v, tb=tb):
                step = ctx.serial(v, pre + "vs")
                v.wait_ge(sin.h, tb * 3 + 2)
                step(v.reciprocal(srow[0:1, :], srow[0:1, :]))
                v.tensor_scalar(srow[0:1, :], srow[0:1, :], 0.5, None,
                                ALU.mult).then_inc(sin.h, 1)
            blk.vector(f_d_ve2)
            with (
                nc.psum_tensor(pre + "psp%d" % tb, [128, 2 * TB], F32) as psp,
                nc.psum_tensor(pre + "pib%d" % tb, [128, TB], F32) as pib,
            ):
                def f_d_te(t, tb=tb, pib=pib):
                    t.wait_ge(sin.h, tb * 3 + 3)
                    if tb >= 1:
                        t.wait_ge(sie.h, tb)
                    ins = None
                    for c in range(NCH):
                        ins = t.matmul(pib[:, c*512:(c+1)*512], C["ONESR32"][:],
                                       srow[0:1, c*512:(c+1)*512],
                                       start=True, stop=True)
                    ins.then_inc(sib.h, 1)
                blk.tensor(f_d_te)
                def f_d_sc2(sc, tb=tb, pib=pib):
                    sc.wait_ge(sib.h, tb + 1)
                    sc.copy(ivbc[:], pib[:]).then_inc(sie.h, 1)
                blk.scalar(f_d_sc2)
                def f_c_sy(sy, tb=tb):
                    sy.wait_ge(shm.h, (tb + 1) * 64)
                    for dt in range(16):
                        i = tb * 16 + dt
                        par = dt % 2
                        if i >= 2:
                            sy.wait_ge(spm.h, i - 1)
                        rwp.inc(sy.dma_start(
                            out=wpv(par).rearrange("d (k o) -> d k o", k=64),
                            in_=wqp_dram[dt].rearrange("k d o -> d k o")), i)
                blk.sync(f_c_sy)
                def f_c_te(t, tb=tb, psp=psp):
                    t.wait_ge(she.h, 64 * (tb + 1))
                    for dt in range(16):
                        i = tb * 16 + dt
                        par = dt % 2
                        rwp.wait(t, i)
                        if i >= 2:
                            t.wait_ge(spe.h, i - 1)
                        ins = None
                        for kh in range(64):
                            for c in range(NCH):
                                ins = t.matmul(
                                    psp[:, par*TB+c*512:par*TB+(c+1)*512],
                                    wpv(par)[:, 128*kh:128*(kh+1)],
                                    hsb[:, kh*TB+c*512:kh*TB+(c+1)*512],
                                    start=(kh == 0), stop=(kh == 63))
                        ins.then_inc(spm.h, 1)
                    spm.n += 16
                blk.tensor(f_c_te)
                def f_c_ve(v, tb=tb, psp=psp):
                    v.wait_ge(sie.h, tb + 1)
                    for dt in range(16):
                        i = tb * 16 + dt
                        par = dt % 2
                        v.wait_ge(spm.h, i + 1)
                        if i >= 2:
                            rps.wait(v, i - 2)
                        v.tensor_mul(V(pbsb, par, TB),
                                     psp[:, par*TB:(par+1)*TB],
                                     ivbc[:]).then_inc(spe.h, 1)
                    spe.n += 16
                blk.vector(f_c_ve)
                def f_c_gp(g, tb=tb, c0=c0):
                    for dt in range(16):
                        i = tb * 16 + dt
                        par = dt % 2
                        g.wait_ge(spe.h, i + 1)
                        rps.inc(g.dma_start(
                            out=p_dram[128*dt:128*(dt+1), c0:c0+TB],
                            in_=V(pbsb, par, TB)), i)
                blk.gpsimd(f_c_gp)
            def f_e_sy(sy, tb=tb, c0=c0):
                rps.wait_all(sy)
                for j in range(NTJ):
                    i = tb * NTJ + j
                    par = j % 2
                    if i >= 2:
                        sy.wait_ge(sra.h, i - 1)
                    if i >= 1:
                        rro.wait(sy, i - 1)
                    rpt.inc(sy.dma_start(out=V(wgsb, par, 2048),
                                         in_=p_dram[0:2048, c0+128*j:c0+128*(j+1)],
                                         transpose=True), i)
                    rxl.inc(sy.dma_start(out=xrsb[:],
                                         in_=x_in_dram[c0+128*j:c0+128*(j+1), :]),
                            i)
            blk.sync(f_e_sy)
            def f_e_ve(v, tb=tb):
                for j in range(NTJ):
                    i = tb * NTJ + j
                    par = j % 2
                    rpt.wait(v, i)
                    rxl.wait(v, i)
                    v.tensor_add(xrsb[:], xrsb[:],
                                 V(wgsb, par, 2048)).then_inc(sra.h, 1)
                sra.n += NTJ
            blk.vector(f_e_ve)
            def f_e_gp(g, tb=tb, c0=c0):
                for j in range(NTJ):
                    i = tb * NTJ + j
                    g.wait_ge(sra.h, i + 1)
                    rro.inc(g.dma_start(out=x_out_dram[c0+128*j:c0+128*(j+1), :],
                                        in_=xrsb[:]), i)
            blk.gpsimd(f_e_gp)

def emit_proj_feat(ctx, xn_dram, w4_dram, out_dram, T, NJ, C, pre="pf",
                   cast_w=True):
    nc, blk = ctx.nc, ctx.blk
    NCH = T // 512
    sxt = ctx.sem(pre + "xt")
    rwl = ctx.ring(pre + "wl", 2)
    rst = ctx.ring(pre + "st", 2)
    smm = ctx.sem(pre + "mm"); sev = ctx.sem(pre + "ev")
    with (
        nc.sbuf_tensor(pre + "xnT", [128, 16 * T], BF16) as xnT,
        nc.sbuf_tensor(pre + "w", [128, 2 * 2048], BF16) as wsb,
        nc.sbuf_tensor(pre + "o", [128, 2 * T], BF16) as osb,
        nc.psum_tensor(pre + "ps", [128, 2 * T], F32) as ps,
    ):
        V = lambda sb, par, w: sb[:, par * w:(par + 1) * w]
        def f_xt(sy):
            for k in range(16):
                sy.dma_start(out=xnT[:, k*T:(k+1)*T],
                             in_=xn_dram[0:T, 128*k:128*(k+1)],
                             transpose=True).then_inc(sxt.h, 16)
            sxt.n += 16 * 16
        blk.sync(f_xt)
        def f_wl(g):
            for j in range(NJ):
                par = j % 2
                if j >= 2:
                    g.wait_ge(smm.h, j - 1)
                rwl.inc(g.dma_start(
                    out=V(wsb, par, 2048).rearrange("d (k o) -> d k o", k=16),
                    in_=w4_dram[j].rearrange("k d o -> d k o")), j)
        (blk.gpsimd if cast_w else blk.sync)(f_wl)
        def f_te(t):
            t.wait_ge(sxt.h, sxt.n)
            for j in range(NJ):
                par = j % 2
                rwl.wait(t, j)
                if j >= 2:
                    t.wait_ge(sev.h, j - 1)
                ins = None
                for k in range(16):
                    for c in range(NCH):
                        ins = t.matmul(ps[:, par*T+c*512:par*T+(c+1)*512],
                                       V(wsb, par, 2048)[:, 128*k:128*(k+1)],
                                       xnT[:, k*T+c*512:k*T+(c+1)*512],
                                       start=(k == 0), stop=(k == 15))
                ins.then_inc(smm.h, 1)
            smm.n += NJ
        blk.tensor(f_te)
        def f_sc(sc):
            for j in range(NJ):
                par = j % 2
                sc.wait_ge(smm.h, j + 1)
                if j >= 2:
                    rst.wait(sc, j - 2)
                sc.copy(V(osb, par, T), ps[:, par*T:(par+1)*T]).then_inc(sev.h, 1)
            sev.n += NJ
        blk.scalar(f_sc)
        def f_st(sy):
            for j in range(NJ):
                par = j % 2
                sy.wait_ge(sev.h, j + 1)
                rst.inc(sy.dma_start(out=out_dram[128*j:128*(j+1), :],
                                     in_=V(osb, par, T)), j)
        blk.sync(f_st)

def prep_shared_A(ins):
    wgu4 = tile_wT(ins["L0_ffn_w_gate_up"])
    wp4 = tile_wT(ins["L0_ffn_w_proj"])
    wqkvz = np.concatenate([ins["L0_attn_W_qkv"], ins["L0_attn_W_z"]], axis=0)
    wqkvz4 = np.ascontiguousarray(tile_wT(wqkvz).transpose(1, 0, 2, 3))
    wab = np.concatenate([ins["L0_attn_W_a"], ins["L0_attn_W_b"]], axis=0)
    wabT = np.ascontiguousarray(wab.T)
    return {"wgu4": wgu4, "wp4": wp4, "wqkvz4": wqkvz4, "wabT": wabT}

def prep_launchA_inputs(ins, core, shared=None):
    b, p = core // 2, core % 2
    x0 = np.ascontiguousarray(ins["L0_attn_hidden_states"][b, 2048*p:2048*(p+1)])
    d = {"x0": x0}
    d.update(shared if shared is not None else prep_shared_A(ins))
    d.update({"c" + k: np.asarray(v) for k, v in make_const_arrays().items()})
    return d

def build_launchA(T=2048):
    import contextlib as _cl
    nc = bass.Bass("TRN2", target_bir_lowering=False)
    cd = {k: nc.dram_tensor("c" + k, CONST_SHAPE[k], CONST_DT[k],
                            kind="ExternalInput").ap() for k in CONST_DT}
    x0 = nc.dram_tensor("x0", [T, D], F32, kind="ExternalInput").ap()
    wgu4 = nc.dram_tensor("wgu4", [16, 128, 128, 128], F32, kind="ExternalInput").ap()
    wp4 = nc.dram_tensor("wp4", [64, 16, 128, 128], F32, kind="ExternalInput").ap()
    wqkvz4 = nc.dram_tensor("wqkvz4", [96, 16, 128, 128], F32, kind="ExternalInput").ap()
    wabT = nc.dram_tensor("wabT", [D, 64], F32, kind="ExternalInput").ap()
    x1 = nc.dram_tensor("x1", [T, D], F32, kind="ExternalOutput").ap()
    wqgu = nc.dram_tensor("wqgu", [128, 16, 128, 128], BF16, kind="ExternalOutput").ap()
    wqp = nc.dram_tensor("wqp", [16, 64, 128, 128], BF16, kind="ExternalOutput").ap()
    qkvz = nc.dram_tensor("qkvz", [96 * 128, T], BF16, kind="ExternalOutput").ap()
    ab = nc.dram_tensor("ab", [T, 64], F32, kind="ExternalOutput").ap()
    xn1 = nc.dram_tensor("xn1", [T, D], BF16)
    xn2 = nc.dram_tensor("xn2", [T, D], BF16)
    p_scr = nc.dram_tensor("p_scr", [D, T], BF16)
    with _cl.ExitStack() as stack:
        with nc.Block() as block:
            ctx = Ctx(nc, block, stack)
            C = load_consts(ctx, cd)
            wab_sb = ctx.sb("wab", [128, 16 * 64], F32)
            swab = ctx.sem("consts")
            def f_wab(sy):
                swab.inc(sy.dma_start(
                    out=wab_sb[:].rearrange("d (k c) -> d k c", k=16),
                    in_=wabT.rearrange("(k d) c -> d k c", d=128)))
            block.sync(f_wab)
            emit_quant(ctx, wgu4, wqgu, 16, 128, C)
            phase_barrier(ctx, rings=["q_wqgu_st"])
            emit_quant(ctx, wp4, wqp, 64, 16, C)
            phase_barrier(ctx, rings=["q_wqp_st"])
            emit_ln(ctx, x0, xn1.ap(), T, C, pre="ln1")
            phase_barrier(ctx, rings=["ln1xo"])
            emit_ffn0(ctx, xn1.ap(), x0, x1, wqgu, wqp, p_scr.ap(), T, C, pre="f0")
            phase_barrier(ctx, rings=["f0ro", "f0ps"])
            emit_ln(ctx, x1, xn2.ap(), T, C, wab_sb=wab_sb, ab_dram=ab, pre="ln2")
            phase_barrier(ctx, rings=["ln2xo", "ln2abo"])
            emit_proj_feat(ctx, xn2.ap(), wqkvz4, qkvz, T, 96, C, pre="pA")
            def fin(sy):
                for rg in ["pAst", "q_wqgu_st", "q_wqp_st", "f0ro", "ln2abo"]:
                    ctx._rings[rg].wait_all(sy)
            block.sync(fin)
    return nc

class Seq:
    def __init__(self, ctx, name):
        self.ctx = ctx
        self.s = ctx.sem(name)
        self.steps = []
    def add(self, eng, fn, dma=False):
        self.steps.append((eng, fn, 16 if dma else 1))
    def emit(self):
        base = self.s.n
        pre = [base]
        for (_, _, a) in self.steps:
            pre.append(pre[-1] + a)
        self.s.n = pre[-1]
        by_eng = {}
        for idx, (en, fn, amt) in enumerate(self.steps):
            by_eng.setdefault(en, []).append((idx, fn, amt))
        for en, lst in by_eng.items():
            def body(eng, lst=lst):
                for idx, fn, amt in lst:
                    eng.wait_ge(self.s.h, pre[idx])
                    ins = fn(eng)
                    ins.then_inc(self.s.h, amt)
            getattr(self.ctx.blk, en)(body)
        self.steps = []

S_FULL = 4096
NCHK = 32
NH = 16
NKH = 8

def make_maskB_arrays(norm_w):
    m_ui = np.triu(np.ones((128, 128), dtype=np.float32)).astype(NPBF)
    m_us_neg = (-np.triu(np.ones((128, 128), dtype=np.float32), 1)).astype(NPBF)
    m_ls_neg = (-np.tril(np.ones((128, 128), dtype=np.float32), -1)).astype(NPBF)
    normbc = np.tile(np.asarray(norm_w, dtype=np.float32)[None, :], (128, 1))
    return {"MUI": m_ui, "MUSN": m_us_neg, "MLSN": m_ls_neg, "NWBC": normbc}

MASKB_DT = {"MUI": BF16, "MUSN": BF16, "MLSN": BF16, "NWBC": F32}

def strided0(ap, nstrip, inner):
    return ap.unsqueeze(2).broadcast_to([128, nstrip, inner])

def bcast0(ap, nstrip, inner):
    return ap.unsqueeze(1).broadcast_to([128, nstrip, inner])

def st16(ap):
    return ap.rearrange("p (s c) -> p s c", s=16)

def emit_B1_conv(ctx, qkvc, cw_sb, qn_sb, kn_sb, kf, vf, C, pre="cv"):
    nc = ctx.nc
    S = S_FULL
    PAD = 8
    with (
        nc.sbuf_tensor(pre + "xi", [128, S + 2 * PAD], BF16) as xin,
        nc.sbuf_tensor(pre + "c", [128, S], F32) as cv,
        nc.sbuf_tensor(pre + "s", [128, S], BF16) as sv,
        nc.sbuf_tensor(pre + "q", [128, S], BF16) as sq,
        nc.sbuf_tensor(pre + "n", [1, S], F32) as nr,
        nc.psum_tensor(pre + "pn", [1, 2048], F32) as ps_n,
        nc.psum_tensor(pre + "pb", [128, 512], F32) as ps_bc,
    ):
        sq_ = Seq(ctx, pre + "seq")
        sq_.add("vector", lambda v: v.memset(xin[:, 0:PAD], 0.0))
        sq_.add("vector", lambda v: v.memset(xin[:, S + PAD:S + 2 * PAD], 0.0))
        for j in range(32):
            sq_.add("sync", lambda sy, j=j: sy.dma_start(
                out=xin[:, PAD:PAD + S], in_=qkvc[128*j:128*(j+1), :]), dma=True)
            sq_.add("vector", lambda v, j=j: v.tensor_scalar(
                cv[:], xin[:, PAD-3:PAD-3+S], cw_sb[:, 4*j:4*j+1], None, ALU.mult))
            for kk in range(1, 4):
                sq_.add("vector", lambda v, j=j, kk=kk: v.scalar_tensor_tensor(
                    cv[:], xin[:, PAD-3+kk:PAD-3+kk+S], cw_sb[:, 4*j+kk:4*j+kk+1],
                    cv[:], ALU.mult, ALU.add))
            sq_.add("scalar", lambda sc: sc.activation(sv[:], cv[:], AF.Silu))
            if j < 16:
                sq_.add("vector", lambda v: v.tensor_mul(sq[:], sv[:], sv[:]))
                for half in range(2):
                    def f_mm(t, half=half):
                        ins = None
                        for c in range(4):
                            ins = t.matmul(ps_n[:, c*512:(c+1)*512], C["ONESC"][:],
                                           sq[:, (4*half+c)*512:(4*half+c+1)*512],
                                           start=True, stop=True)
                        return ins
                    sq_.add("tensor", f_mm)
                    sq_.add("scalar", lambda sc, half=half: sc.copy(
                        nr[0:1, 2048*half:2048*(half+1)], ps_n[0:1, :]))
                sq_.add("scalar", lambda sc: sc.activation(nr[0:1, :], nr[0:1, :],
                                                           AF.Sqrt))
                sq_.add("vector", lambda v: v.tensor_scalar_max(nr[0:1, :],
                                                                nr[0:1, :], 1e-6))
                sq_.add("vector", lambda v: v.reciprocal(nr[0:1, :], nr[0:1, :]))
                if j < 8:
                    sq_.add("vector", lambda v: v.tensor_scalar(
                        nr[0:1, :], nr[0:1, :], 0.088388347648318447, None,
                        ALU.mult))
                dst = qn_sb if j < 8 else kn_sb
                hh = j % 8
                for c in range(8):
                    sq_.add("tensor", lambda t, c=c: t.matmul(
                        ps_bc[:], C["ONESR32"][:], nr[0:1, 512*c:512*(c+1)],
                        start=True, stop=True))
                    sq_.add("vector", lambda v, c=c, dst=dst, hh=hh: v.tensor_mul(
                        dst[:, hh*S+512*c:hh*S+512*(c+1)],
                        sv[:, 512*c:512*(c+1)], ps_bc[:]))
                if 8 <= j:
                    sq_.add("gpsimd", lambda g, hh=hh: g.dma_start(
                        out=kf[128*hh:128*(hh+1), :],
                        in_=kn_sb[:, hh*S:(hh+1)*S]), dma=True)
            else:
                sq_.add("gpsimd", lambda g, j=j: g.dma_start(
                    out=vf[128*(j-16):128*(j-15), :], in_=sv[:]), dma=True)
        sq_.emit()

def emit_B2_gb(ctx, abf, adt_sb, gba, gba_d, C, pre="gb"):
    nc = ctx.nc
    S = S_FULL
    with (
        nc.sbuf_tensor(pre + "ab", [48, S], F32) as ab_sb,
        nc.sbuf_tensor(pre + "s1", [16, S], F32) as s1,
        nc.sbuf_tensor(pre + "s2", [16, S], F32) as s2,
        nc.sbuf_tensor(pre + "ea", [16, 2], F32) as ea,
    ):
        t0 = gba[0:16, :]
        sq_ = Seq(ctx, pre + "seq")
        sq_.add("sync", lambda sy: sy.dma_start(out=ab_sb[:], in_=abf), dma=True)
        sq_.add("vector", lambda v: v.tensor_scalar_add(t0, ab_sb[0:16, :],
                                                        adt_sb[:, 1:2]))
        sq_.add("scalar", lambda sc: sc.activation(s1[:], t0, AF.Abs))
        sq_.add("scalar", lambda sc: sc.activation(s1[:], s1[:], AF.Exp,
                                                   scale=-1.0))
        sq_.add("vector", lambda v: v.tensor_scalar_add(s1[:], s1[:], 1.0))
        sq_.add("scalar", lambda sc: sc.activation(s1[:], s1[:], AF.Ln))
        sq_.add("vector", lambda v: v.tensor_scalar_max(s2[:], t0, 0.0))
        sq_.add("vector", lambda v: v.tensor_add(s2[:], s2[:], s1[:]))
        sq_.add("scalar", lambda sc: sc.activation(ea[:, 0:1], adt_sb[:, 0:1],
                                                   AF.Exp))
        sq_.add("vector", lambda v: v.tensor_scalar(ea[:, 1:2], ea[:, 0:1],
                                                    -1.0, None, ALU.mult))
        sq_.add("vector", lambda v: v.tensor_scalar(s2[:], s2[:], ea[:, 1:2],
                                                    None, ALU.mult))
        sq_.add("scalar", lambda sc: sc.activation(gba[32:48, :], ab_sb[32:48, :],
                                                   AF.Sigmoid))
        for cc in range(NCHK):
            sq_.add("vector", lambda v, cc=cc: v.tensor_tensor_scan(
                gba[0:16, 128*cc:128*(cc+1)], s2[:, 128*cc:128*(cc+1)],
                s2[:, 128*cc:128*(cc+1)], 0.0, ALU.add, ALU.bypass))
        sq_.add("gpsimd", lambda g: g.dma_start(out=gba_d, in_=gba[0:16, :]),
                dma=True)
        sq_.emit()

def emit_B3_pass1(ctx, qn_sb, kn_sb, gba, gba_d, kf, vf, tw, m1s, aes, khs,
                  C, M_, pre="p1"):
    nc = ctx.nc
    S = S_FULL
    import contextlib as _cl
    with _cl.ExitStack() as es:
        sb = lambda n, sh, dt: es.enter_context(nc.sbuf_tensor(pre + n, sh, dt))
        pstensor = lambda n, sh: es.enter_context(nc.psum_tensor(pre + n, sh, F32))
        grow = sb("gr", [1, 2048], F32); gRow = sb("gR", [1, 16], F32)
        kt = sb("kt", [128, 1024], BF16); vt = sb("vt", [128, 2048], BF16)
        gbT = sb("gbT", [128, 48], F32); ebT = sb("ebT", [128, 16], F32)
        gbc = sb("gbc", [128, 16], F32); e0 = sb("e0", [128, 16], F32)
        egm = sb("egm", [128, 16], F32)
        e1 = sb("e1", [128, 2048], BF16)
        tg = sb("tg", [128, 2048], BF16)
        p_T = sb("pT", [128, 2048], BF16); p_p = sb("pp", [128, 2048], BF16)
        Tp = sb("Tp", [128, 2048], BF16); TT = sb("TT", [128, 2048], BF16)
        aesb = sb("ae", [128, 2048], BF16); kbar = sb("kb", [128, 2048], BF16)
        khat = sb("kh", [128, 2048], BF16)
        m1sb = khat
        twsb = aesb
        psA = pstensor("psA", [128, 2048]); psB = pstensor("psB", [128, 2048])
        ps32 = psA[:, 0:48]; ps16 = psA[:, 48:64]
        betaT = gbT[:, 32:48]
        ktrep = kt[:].rearrange("p (kh c) -> p kh c", kh=8).unsqueeze(2
                   ).broadcast_to([128, 8, 2, 128])
        r4 = lambda ap: ap.rearrange("p (kh r c) -> p kh r c", kh=8, r=2)
        s0r4 = lambda col16: strided0(col16, 16, 128).rearrange(
            "p (kh r) c -> p kh r c", kh=8)
        for cc in range(NCHK):
            sq_ = Seq(ctx, pre + "seq")
            c0 = 128 * cc
            sq_.add("sync", lambda sy, c0=c0: sy.dma_start(
                out=grow[:].rearrange("a (h c) -> a h c", h=16),
                in_=gba_d[0:16, c0:c0+128].unsqueeze(0)), dma=True)
            sq_.add("sync", lambda sy, c0=c0: sy.dma_start(
                out=gRow[:].unsqueeze(2),
                in_=gba_d[0:16, c0+127:c0+128].unsqueeze(0),
                allow_slow_non_contiguous=True), dma=True)
            sq_.add("sync", lambda sy, c0=c0: sy.dma_start(
                out=kt[:], in_=kf[0:1024, c0:c0+128], transpose=True), dma=True)
            sq_.add("sync", lambda sy, c0=c0: sy.dma_start(
                out=vt[:], in_=vf[0:2048, c0:c0+128], transpose=True), dma=True)
            sq_.add("tensor", lambda t, c0=c0: t.transpose(
                ps32, gba[0:48, c0:c0+128], C["IDENT"][0:48, 0:48]))
            sq_.add("scalar", lambda sc: sc.copy(gbT[:], ps32))
            sq_.add("scalar", lambda sc: sc.activation(ebT[:], gbT[:, 0:16],
                                                       AF.Exp))
            sq_.add("tensor", lambda t: t.matmul(ps16, C["ONESR32"][:],
                                                 gRow[:], start=True, stop=True))
            sq_.add("scalar", lambda sc: sc.copy(gbc[:], ps16))
            sq_.add("vector", lambda v: v.tensor_sub(e0[:], gbc[:],
                                                     gbT[:, 0:16]))
            sq_.add("scalar", lambda sc: sc.activation(egm[:], e0[:], AF.Exp))
            def f_ddmm(t):
                ins = None
                for c in range(4):
                    ins = t.matmul(psA[:, c*512:(c+1)*512], C["ONESR32"][:],
                                   grow[0:1, c*512:(c+1)*512],
                                   start=True, stop=True)
                return ins
            sq_.add("tensor", f_ddmm)
            def f_dd1(v):
                ins = None
                for h in range(16):
                    sl = slice(128*h, 128*(h+1))
                    ins = v.tensor_scalar(psB[:, sl], psA[:, sl],
                                          gbT[:, h:h+1], 0.0,
                                          ALU.subtract, ALU.min)
                return ins
            sq_.add("vector", f_dd1)
            sq_.add("scalar", lambda sc: sc.activation(e1[:], psB[:], AF.Exp))
            def f_dd2(v):
                ins = None
                for h in range(16):
                    sl = slice(128*h, 128*(h+1))
                    ins = v.tensor_scalar(psB[:, sl], psA[:, sl],
                                          gbT[:, h:h+1], 0.0,
                                          ALU.subtract, ALU.max)
                return ins
            sq_.add("vector", f_dd2)
            sq_.add("scalar", lambda sc: sc.activation(p_p[:], psB[:], AF.Exp,
                                                       scale=-1.0))
            def f_gmm(t, c0=c0):
                ins = None
                for h in range(16):
                    kc = kn_sb[:, (h // 2)*S + c0:(h // 2)*S + c0 + 128]
                    ins = t.matmul(psA[:, 128*h:128*(h+1)], kc, kc,
                                   start=True, stop=True)
                return ins
            sq_.add("tensor", f_gmm)
            sq_.add("vector", lambda v: v.tensor_mul(
                st16(tg[:]), st16(e1[:]), bcast0(M_["MUSN"][:], 16, 128)))
            sq_.add("vector", lambda v: v.tensor_mul(tg[:], tg[:], psA[:]))
            sq_.add("vector", lambda v: v.tensor_mul(
                st16(p_T[:]), st16(tg[:]), strided0(betaT, 16, 128)))
            sq_.add("vector", lambda v: v.tensor_mul(
                st16(tg[:]), st16(p_p[:]), bcast0(M_["MLSN"][:], 16, 128)))
            sq_.add("vector", lambda v: v.tensor_mul(tg[:], tg[:], psA[:]))
            sq_.add("vector", lambda v: v.tensor_mul(
                st16(p_p[:]), st16(tg[:]), strided0(betaT, 16, 128)))
            def f_amm(t, c0=c0):
                ins = None
                for h in range(16):
                    kc = kn_sb[:, (h // 2)*S + c0:(h // 2)*S + c0 + 128]
                    qc = qn_sb[:, (h // 2)*S + c0:(h // 2)*S + c0 + 128]
                    ins = t.matmul(psA[:, 128*h:128*(h+1)], kc, qc,
                                   start=True, stop=True)
                return ins
            sq_.add("tensor", f_amm)
            sq_.add("vector", lambda v: v.tensor_mul(
                st16(tg[:]), st16(e1[:]), bcast0(M_["MUI"][:], 16, 128)))
            sq_.add("vector", lambda v: v.tensor_mul(aesb[:], tg[:], psA[:]))
            sq_.add("gpsimd", lambda g, cc=cc: g.dma_start(out=aes[cc],
                                                           in_=aesb[:]),
                    dma=True)
            sq_.add("vector", lambda v: v.tensor_mul(r4(kbar[:]), ktrep,
                                                     s0r4(ebT[:])))
            sq_.add("vector", lambda v: v.tensor_mul(r4(khat[:]), ktrep,
                                                     s0r4(egm[:])))
            sq_.add("gpsimd", lambda g, cc=cc: g.dma_start(out=khs[cc],
                                                           in_=khat[:]),
                    dma=True)
            sq_.add("vector", lambda v: v.tensor_copy(
                st16(Tp[:]), bcast0(C["IDENTB"][:], 16, 128)))
            sq_.add("vector", lambda v: v.tensor_copy(
                st16(TT[:]), bcast0(C["IDENTB"][:], 16, 128)))
            for it in range(7):
                def f_y1(t):
                    ins = None
                    for h in range(16):
                        sl = slice(128*h, 128*(h+1))
                        ins = t.matmul(psA[:, sl], p_T[:, sl], Tp[:, sl],
                                       start=True, stop=True)
                    return ins
                sq_.add("tensor", f_y1)
                def f_y2(t):
                    ins = None
                    for h in range(16):
                        sl = slice(128*h, 128*(h+1))
                        ins = t.matmul(psB[:, sl], Tp[:, sl], p_T[:, sl],
                                       start=True, stop=True)
                    return ins
                sq_.add("tensor", f_y2)
                sq_.add("vector", lambda v: v.tensor_add(Tp[:], Tp[:], psA[:]))
                sq_.add("vector", lambda v: v.tensor_add(TT[:], TT[:], psB[:]))
                if it < 6:
                    def f_q1(t):
                        ins = None
                        for h in range(16):
                            sl = slice(128*h, 128*(h+1))
                            ins = t.matmul(psA[:, sl], p_T[:, sl], p_p[:, sl],
                                           start=True, stop=True)
                        return ins
                    sq_.add("tensor", f_q1)
                    def f_q2(t):
                        ins = None
                        for h in range(16):
                            sl = slice(128*h, 128*(h+1))
                            ins = t.matmul(psB[:, sl], p_p[:, sl], p_T[:, sl],
                                           start=True, stop=True)
                        return ins
                    sq_.add("tensor", f_q2)
                    sq_.add("vector", lambda v: v.tensor_copy(p_p[:], psA[:]))
                    sq_.add("scalar", lambda sc: sc.copy(p_T[:], psB[:]))
            def f_wmm(t):
                ins = None
                for h in range(16):
                    sl = slice(128*h, 128*(h+1))
                    ins = t.matmul(psA[:, sl], kbar[:, sl], TT[:, sl],
                                   start=True, stop=True)
                return ins
            sq_.add("tensor", f_wmm)
            sq_.add("scalar", lambda sc: sc.copy(twsb[:], psA[:]))
            sq_.add("gpsimd", lambda g, cc=cc: g.dma_start(out=tw[cc],
                                                           in_=twsb[:]),
                    dma=True)
            def f_m1mm(t):
                ins = None
                for h in range(16):
                    sl = slice(128*h, 128*(h+1))
                    ins = t.matmul(psB[:, sl], TT[:, sl], vt[:, sl],
                                   start=True, stop=True)
                return ins
            sq_.add("tensor", f_m1mm)
            sq_.add("vector", lambda v: v.tensor_mul(
                st16(m1sb[:]), st16(psB[:]), strided0(betaT, 16, 128)))
            sq_.add("gpsimd", lambda g, cc=cc: g.dma_start(out=m1s[cc],
                                                           in_=m1sb[:]),
                    dma=True)
            sq_.emit()

def emit_B4_pass2(ctx, qn_sb, gba, gba_d, tw, m1s, aes, khs, ztok, og, C, M_,
                  pre="p2"):
    nc = ctx.nc
    S = S_FULL
    import contextlib as _cl
    with _cl.ExitStack() as es:
        sb = lambda n, sh, dt: es.enter_context(nc.sbuf_tensor(pre + n, sh, dt))
        pstensor = lambda n, sh: es.enter_context(nc.psum_tensor(pre + n, sh, F32))
        twc = sb("tw", [128, 2048], BF16); m1c = sb("m1", [128, 2048], BF16)
        aec = sb("ae", [128, 2048], BF16); khc = sb("kh", [128, 2048], BF16)
        zc = sb("z", [128, 2048], BF16); gRow = sb("gR", [1, 16], F32)
        gbT = sb("gbT", [128, 48], F32); ebT = sb("ebT", [128, 16], F32)
        gbc = sb("gbc", [128, 16], F32); ebC = sb("ebC", [128, 16], F32)
        Ssb = sb("S", [128, 2048], F32); Sbf = sb("Sb", [128, 2048], BF16)
        tmp = sb("tmp", [128, 2048], BF16); delta = sb("dl", [128, 2048], BF16)
        obuf = sb("o", [128, 2048], F32); oq = sb("oq", [128, 2048], F32)
        rr = sb("rr", [128, 16], F32); opb = sb("op", [128, 2048], BF16)
        psA = pstensor("psA", [128, 2048]); psB = pstensor("psB", [128, 2048])
        ps32 = psA[:, 0:48]; ps16 = psA[:, 48:64]
        betaT = gbT[:, 32:48]
        sq0 = Seq(ctx, pre + "init")
        sq0.add("vector", lambda v: v.memset(Ssb[:], 0.0))
        sq0.add("vector", lambda v: v.memset(Sbf[:], 0.0))
        sq0.emit()
        for cc in range(NCHK):
            sq_ = Seq(ctx, pre + "seq")
            c0 = 128 * cc
            sq_.add("sync", lambda sy, cc=cc: sy.dma_start(out=twc[:],
                                                           in_=tw[cc]), dma=True)
            sq_.add("sync", lambda sy, cc=cc: sy.dma_start(out=m1c[:],
                                                           in_=m1s[cc]), dma=True)
            sq_.add("sync", lambda sy, cc=cc: sy.dma_start(out=aec[:],
                                                           in_=aes[cc]), dma=True)
            sq_.add("sync", lambda sy, cc=cc: sy.dma_start(out=khc[:],
                                                           in_=khs[cc]), dma=True)
            sq_.add("sync", lambda sy, c0=c0: sy.dma_start(
                out=zc[:], in_=ztok[c0:c0+128, :]), dma=True)
            sq_.add("sync", lambda sy, c0=c0: sy.dma_start(
                out=gRow[:].unsqueeze(2),
                in_=gba_d[0:16, c0+127:c0+128].unsqueeze(0),
                allow_slow_non_contiguous=True), dma=True)
            sq_.add("tensor", lambda t, c0=c0: t.transpose(
                ps32, gba[0:48, c0:c0+128], C["IDENT"][0:48, 0:48]))
            sq_.add("scalar", lambda sc: sc.copy(gbT[:], ps32))
            sq_.add("scalar", lambda sc: sc.activation(ebT[:], gbT[:, 0:16],
                                                       AF.Exp))
            sq_.add("tensor", lambda t: t.matmul(ps16, C["ONESR32"][:],
                                                 gRow[:], start=True, stop=True))
            sq_.add("scalar", lambda sc: sc.copy(gbc[:], ps16))
            sq_.add("scalar", lambda sc: sc.activation(ebC[:], gbc[:], AF.Exp))
            def f_y1(t):
                ins = None
                for h in range(16):
                    sl = slice(128*h, 128*(h+1))
                    ins = t.matmul(psA[:, sl], twc[:, sl], Sbf[:, sl],
                                   start=True, stop=True)
                return ins
            sq_.add("tensor", f_y1)
            sq_.add("vector", lambda v: v.tensor_mul(
                st16(tmp[:]), st16(psA[:]), strided0(betaT, 16, 128)))
            sq_.add("vector", lambda v: v.tensor_sub(delta[:], m1c[:], tmp[:]))
            def f_o1(t, c0=c0):
                ins = None
                for h in range(16):
                    sl = slice(128*h, 128*(h+1))
                    qc = qn_sb[:, (h // 2)*S + c0:(h // 2)*S + c0 + 128]
                    ins = t.matmul(psB[:, sl], qc, Sbf[:, sl],
                                   start=True, stop=True)
                return ins
            sq_.add("tensor", f_o1)
            def f_o2(t):
                ins = None
                for h in range(16):
                    sl = slice(128*h, 128*(h+1))
                    ins = t.matmul(psA[:, sl], aec[:, sl], delta[:, sl],
                                   start=True, stop=True)
                return ins
            sq_.add("tensor", f_o2)
            sq_.add("vector", lambda v: v.tensor_mul(
                st16(obuf[:]), st16(psB[:]), strided0(ebT[:], 16, 128)))
            sq_.add("vector", lambda v: v.tensor_add(obuf[:], obuf[:], psA[:]))
            def f_sinc(t):
                ins = None
                for h in range(16):
                    sl = slice(128*h, 128*(h+1))
                    ins = t.matmul(psB[:, sl], khc[:, sl], delta[:, sl],
                                   start=True, stop=True)
                return ins
            sq_.add("tensor", f_sinc)
            sq_.add("vector", lambda v: v.tensor_mul(
                st16(Ssb[:]), st16(Ssb[:]), strided0(ebC[:], 16, 128)))
            sq_.add("vector", lambda v: v.tensor_add(Ssb[:], Ssb[:], psB[:]))
            sq_.add("vector", lambda v: v.tensor_copy(Sbf[:], Ssb[:]))
            sq_.add("vector", lambda v: v.tensor_mul(oq[:], obuf[:], obuf[:]))
            sq_.add("vector", lambda v: v.tensor_reduce(
                rr[:], oq[:].rearrange("p (s c) -> p s c", s=16), AX.X, ALU.add))
            sq_.add("vector", lambda v: v.tensor_scalar(rr[:], rr[:],
                                                        1.0 / 128.0, EPS_L0,
                                                        ALU.mult, ALU.add))
            sq_.add("scalar", lambda sc: sc.activation(rr[:], rr[:], AF.Sqrt))
            sq_.add("vector", lambda v: v.reciprocal(rr[:], rr[:]))
            sq_.add("vector", lambda v: v.tensor_mul(
                st16(obuf[:]), st16(obuf[:]), strided0(rr[:], 16, 128)))
            sq_.add("vector", lambda v: v.tensor_mul(
                st16(obuf[:]), st16(obuf[:]), bcast0(M_["NWBC"][:], 16, 128)))
            sq_.add("scalar", lambda sc: sc.activation(zc[:], zc[:], AF.Silu))
            sq_.add("vector", lambda v: v.tensor_mul(opb[:], obuf[:], zc[:]))
            sq_.add("gpsimd", lambda g, c0=c0: g.dma_start(
                out=og[c0:c0+128, :], in_=opb[:]), dma=True)
            sq_.emit()

def emit_B5_wout(ctx, og, woT, outp, C, pre="b5"):
    nc = ctx.nc
    S = S_FULL
    with (
        nc.sbuf_tensor(pre + "ot", [128, 16 * 2048], BF16) as ots,
        nc.sbuf_tensor(pre + "wo", [128, 16 * 2048], BF16) as wos,
        nc.sbuf_tensor(pre + "ev", [128, 2048], F32) as ev,
        nc.psum_tensor(pre + "ps", [128, 2048], F32) as ps,
    ):
        sq_ = Seq(ctx, pre + "seq")
        for kch in range(16):
            sq_.add("gpsimd", lambda g, kch=kch: g.dma_start(
                out=wos[:, kch*2048:(kch+1)*2048],
                in_=woT[128*kch:128*(kch+1), :]), dma=True)
        for half in range(2):
            t0 = half * 2048
            for kch in range(16):
                sq_.add("sync", lambda sy, kch=kch, t0=t0: sy.dma_start(
                    out=ots[:, kch*2048:(kch+1)*2048],
                    in_=og[t0:t0+2048, 128*kch:128*(kch+1)], transpose=True),
                    dma=True)
            for tt in range(16):
                def f_mm(t, tt=tt):
                    ins = None
                    for kch in range(16):
                        lhs = ots[:, kch*2048+128*tt:kch*2048+128*(tt+1)]
                        for c in range(4):
                            ins = t.matmul(ps[:, c*512:(c+1)*512], lhs,
                                           wos[:, kch*2048+c*512:kch*2048+(c+1)*512],
                                           start=(kch == 0), stop=(kch == 15))
                    return ins
                sq_.add("tensor", f_mm)
                sq_.add("vector", lambda v: v.tensor_copy(ev[:], ps[:]))
                sq_.add("gpsimd", lambda g, tt=tt, t0=t0: g.dma_start(
                    out=outp[t0+128*tt:t0+128*(tt+1), :], in_=ev[:]), dma=True)
        sq_.emit()
        ctx.b5_done = ctx.sem(pre + "seq")

def build_launchB(debug=False):
    import contextlib as _cl
    S = S_FULL
    nc = bass.Bass("TRN2", target_bir_lowering=False)
    KIND = "ExternalOutput" if debug else "Internal"
    cd = {k: nc.dram_tensor("c" + k, CONST_SHAPE[k], CONST_DT[k],
                            kind="ExternalInput").ap() for k in CONST_DT}
    md = {k: nc.dram_tensor("m" + k, [128, 128], MASKB_DT[k],
                            kind="ExternalInput").ap() for k in MASKB_DT}
    qkvc = nc.dram_tensor("qkvc", [4096, S], BF16, kind="ExternalInput").ap()
    convw = nc.dram_tensor("convw", [4096, 4], F32, kind="ExternalInput").ap()
    abf = nc.dram_tensor("abf", [48, S], F32, kind="ExternalInput").ap()
    adt = nc.dram_tensor("adt", [16, 2], F32, kind="ExternalInput").ap()
    ztok = nc.dram_tensor("ztok", [S, 2048], BF16, kind="ExternalInput").ap()
    woT = nc.dram_tensor("woT", [2048, 2048], F32, kind="ExternalInput").ap()
    outp = nc.dram_tensor("outp", [S, 2048], F32, kind="ExternalOutput").ap()
    kf = nc.dram_tensor("kf_scr", [1024, S], BF16, kind=KIND)
    vf = nc.dram_tensor("vf_scr", [2048, S], BF16, kind=KIND)
    gba_d = nc.dram_tensor("gba_d", [16, S], F32, kind=KIND)
    tw = nc.dram_tensor("tw_scr", [NCHK, 128, 2048], BF16, kind=KIND)
    m1s = nc.dram_tensor("m1_scr", [NCHK, 128, 2048], BF16, kind=KIND)
    aes = nc.dram_tensor("ae_scr", [NCHK, 128, 2048], BF16, kind=KIND)
    khs = nc.dram_tensor("kh_scr", [NCHK, 128, 2048], BF16, kind=KIND)
    og = nc.dram_tensor("og_scr", [S, 2048], BF16, kind=KIND)
    with _cl.ExitStack() as stack:
        with nc.Block() as block:
            ctx = Ctx(nc, block, stack)
            C = load_consts(ctx, cd)
            M_ = {}
            sc0 = ctx.sem("consts")
            def f_m(sy):
                for k in MASKB_DT:
                    sb = ctx.sb("m_" + k, [128, 128], MASKB_DT[k])
                    M_[k] = sb
                    sc0.inc(sy.dma_start(out=sb[:], in_=md[k]))
            block.sync(f_m)
            cw_sb = ctx.sb("cw_s", [128, 32 * 4], F32)
            adt_sb = ctx.sb("adt_s", [16, 2], F32)
            def f_m2(sy):
                sc0.inc(sy.dma_start(
                    out=cw_sb[:].rearrange("p (j k) -> p j k", j=32),
                    in_=convw.rearrange("(j p) k -> p j k", p=128)))
                sc0.inc(sy.dma_start(out=adt_sb[:], in_=adt))
            block.sync(f_m2)
            def f_cw(sy):
                sy.wait_ge(sc0.h, sc0.n)
            block.sync(f_cw)
            gba = ctx.sb("gba", [48, S], F32)
            with nc.sbuf_tensor("qn", [128, NKH * S], BF16) as qn_sb:
                with nc.sbuf_tensor("kn", [128, NKH * S], BF16) as kn_sb:
                    emit_B1_conv(ctx, qkvc, cw_sb, qn_sb, kn_sb, kf.ap(),
                                 vf.ap(), C)
                    phase_barrier(ctx, semnames=["cvseq"])
                    emit_B2_gb(ctx, abf, adt_sb, gba, gba_d.ap(), C)
                    phase_barrier(ctx, semnames=["gbseq"])
                    emit_B3_pass1(ctx, qn_sb, kn_sb, gba, gba_d.ap(), kf.ap(),
                                  vf.ap(), tw.ap(), m1s.ap(), aes.ap(),
                                  khs.ap(), C, M_)
                    phase_barrier(ctx, semnames=["p1seq"])
                emit_B4_pass2(ctx, qn_sb, gba, gba_d.ap(), tw.ap(), m1s.ap(),
                              aes.ap(), khs.ap(), ztok, og.ap(), C, M_)
                phase_barrier(ctx, semnames=["p2seq", "p2init"])
            emit_B5_wout(ctx, og.ap(), woT, outp, C)
            def fin(sy):
                s = ctx.sem("b5seq")
                sy.wait_ge(s.h, s.n)
            block.sync(fin)
    return nc

def prep_launchB_inputs(ins, core, a_outs):
    b, p = core // 2, core % 2
    ce, co = 2 * b, 2 * b + 1
    def getq(c):
        q = a_outs[c]["qkvz"]
        return q.view(NPBF) if q.dtype == np.uint16 else q
    qe, qo = getq(ce), getq(co)
    full = np.concatenate([qe, qo], axis=1)
    kh0 = 8 * p
    vh0 = 16 * p
    rows = np.concatenate([
        full[128*kh0:128*(kh0+8)],
        full[2048+128*kh0:2048+128*(kh0+8)],
        full[4096+128*vh0:4096+128*(vh0+16)],
    ], axis=0)
    qkvc = np.ascontiguousarray(rows)
    cw = ins["L0_attn_conv_w"]
    convw = np.ascontiguousarray(np.concatenate([
        cw[128*kh0:128*(kh0+8)], cw[2048+128*kh0:2048+128*(kh0+8)],
        cw[4096+128*vh0:4096+128*(vh0+16)]], axis=0))
    abe, abo = a_outs[ce]["ab"], a_outs[co]["ab"]
    abfull = np.concatenate([abe, abo], axis=0)
    a_m = abfull[:, 16*p:16*(p+1)]
    b_m = abfull[:, 32+16*p:32+16*(p+1)]
    abf = np.zeros((48, 4096), dtype=np.float32)
    abf[0:16] = a_m.T
    abf[32:48] = b_m.T
    adt = np.ascontiguousarray(np.stack(
        [ins["L0_attn_A_log"][16*p:16*(p+1)],
         ins["L0_attn_dt_bias"][16*p:16*(p+1)]], axis=1))
    z_m = np.ascontiguousarray(full[8192+128*vh0:8192+128*(vh0+16)].T)
    woT = np.ascontiguousarray(ins["L0_attn_W_out"].T[2048*p:2048*(p+1), :])
    d = {"qkvc": qkvc, "convw": convw, "abf": abf, "adt": adt,
         "ztok": z_m, "woT": woT}
    d.update({"c" + k: np.asarray(v) for k, v in make_const_arrays().items()})
    d.update({"m" + k: np.asarray(v) for k, v in
              make_maskB_arrays(ins["L0_attn_norm_w"]).items()})
    return d

def emit_resid(ctx, terms, x_out_dram, T, C, pre="rs", consts_add=()):
    nc = ctx.nc
    with (
        nc.sbuf_tensor(pre + "a", [128, D], F32) as acc,
        nc.sbuf_tensor(pre + "b", [128, D], F32) as tb,
    ):
        sq_ = Seq(ctx, pre + "seq")
        for i in range(T // 128):
            first = True
            for dram, dt in terms:
                dst = acc if first else tb
                sq_.add("sync", lambda sy, dram=dram, i=i, dst=dst: sy.dma_start(
                    out=dst[:], in_=dram[128*i:128*(i+1), :]), dma=True)
                if not first:
                    sq_.add("vector", lambda v: v.tensor_add(acc[:], acc[:],
                                                             tb[:]))
                first = False
            for cb in consts_add:
                sq_.add("vector", lambda v, cb=cb: v.tensor_add(acc[:], acc[:],
                                                                cb[:]))
            sq_.add("gpsimd", lambda g, i=i: g.dma_start(
                out=x_out_dram[128*i:128*(i+1), :], in_=acc[:]), dma=True)
        sq_.emit()

def emit_ln_double(ctx, x_dram, xn_dram, xln_dram, lng_bc, lnb_bc, T, C,
                   pre="l2"):
    nc = ctx.nc
    with (
        nc.sbuf_tensor(pre + "xs", [128, D], F32) as xs,
        nc.sbuf_tensor(pre + "xq", [128, D], F32) as xq,
        nc.sbuf_tensor(pre + "st", [128, 16], F32) as st,
        nc.sbuf_tensor(pre + "xf", [128, D], F32) as xf,
        nc.sbuf_tensor(pre + "xb", [128, D], BF16) as xb,
    ):
        sq_ = Seq(ctx, pre + "seq")
        def stats(dvals, s0):
            sq_.add("vector", lambda v: v.reduce_sum(st[:, s0:s0+1], dvals,
                                                     AX.X))
            sq_.add("scalar", lambda sc: sc.activation(xq[:], dvals, AF.Square))
            sq_.add("vector", lambda v: v.reduce_sum(st[:, s0+1:s0+2], xq[:],
                                                     AX.X))
            sq_.add("vector", lambda v: v.tensor_scalar(st[:, s0:s0+1],
                st[:, s0:s0+1], 1.0 / D, None, ALU.mult))
            sq_.add("vector", lambda v: v.tensor_scalar(st[:, s0+1:s0+2],
                st[:, s0+1:s0+2], 1.0 / D, None, ALU.mult))
            sq_.add("vector", lambda v: v.tensor_mul(st[:, s0+2:s0+3],
                st[:, s0:s0+1], st[:, s0:s0+1]))
            sq_.add("vector", lambda v: v.scalar_tensor_tensor(st[:, s0+1:s0+2],
                st[:, s0+1:s0+2], EPS_LN, st[:, s0+2:s0+3], ALU.add,
                ALU.subtract))
            sq_.add("scalar", lambda sc: sc.activation(st[:, s0+1:s0+2],
                st[:, s0+1:s0+2], AF.Sqrt))
            sq_.add("vector", lambda v: v.reciprocal(st[:, s0+3:s0+4],
                                                     st[:, s0+1:s0+2]))
        for i in range(T // 128):
            sq_.add("sync", lambda sy, i=i: sy.dma_start(
                out=xs[:], in_=x_dram[128*i:128*(i+1), :]), dma=True)
            stats(xs[:], 0)
            sq_.add("vector", lambda v: v.tensor_scalar(xf[:], xs[:],
                st[:, 0:1], st[:, 3:4], ALU.subtract, ALU.mult))
            sq_.add("vector", lambda v: v.tensor_copy(xb[:], xf[:]))
            sq_.add("gpsimd", lambda g, i=i: g.dma_start(
                out=xn_dram[128*i:128*(i+1), :], in_=xb[:]), dma=True)
            stats(xf[:], 4)
            sq_.add("vector", lambda v: v.tensor_scalar(xf[:], xf[:],
                st[:, 4:5], st[:, 7:8], ALU.subtract, ALU.mult))
            sq_.add("vector", lambda v: v.tensor_mul(xf[:], xf[:], lng_bc[:]))
            sq_.add("vector", lambda v: v.tensor_add(xf[:], xf[:], lnb_bc[:]))
            sq_.add("vector", lambda v: v.tensor_copy(xb[:], xf[:]))
            sq_.add("gpsimd", lambda g, i=i: g.dma_start(
                out=xln_dram[128*i:128*(i+1), :], in_=xb[:]), dma=True)
        sq_.emit()

def emit_xnT_load(ctx, xn_dram, xnT, T, pre="xt"):
    sq_ = Seq(ctx, pre + "seq")
    for k in range(16):
        sq_.add("sync", lambda sy, k=k: sy.dma_start(
            out=xnT[:, k*T:(k+1)*T], in_=xn_dram[0:T, 128*k:128*(k+1)],
            transpose=True), dma=True)
    sq_.emit()

def emit_moe(ctx, xnT, x_in_dram, x_out_dram, wrT, wg4, wu4, wdTs, wsg4, wsu4,
             wsdT, acc_dram, wr_dram, T, C, pre="mo"):
    nc = ctx.nc
    NT = T // 128
    with (
        nc.sbuf_tensor(pre + "wr", [128, 16 * 8], BF16) as wrsb,
        nc.sbuf_tensor(pre + "sx", [128, 8], F32) as sx,
        nc.sbuf_tensor(pre + "s1", [128, 4], F32) as s1,
        nc.sbuf_tensor(pre + "row", [128, 8], F32) as row,
        nc.sbuf_tensor(pre + "wf", [8, T], F32) as wfT,
        nc.psum_tensor(pre + "p8", [128, 8], F32) as ps8,
        nc.psum_tensor(pre + "pt8", [8, 128], F32) as pst8,
    ):
        sq_ = Seq(ctx, pre + "rt")
        sq_.add("gpsimd", lambda g: g.dma_start(
            out=wrsb[:].rearrange("d (k e) -> d k e", k=16),
            in_=wrT.rearrange("(k d) e -> d k e", d=128)), dma=True)
        for i in range(NT):
            def f_rmm(t, i=i):
                ins = None
                for k in range(16):
                    ins = t.matmul(ps8[:], xnT[:, k*T+128*i:k*T+128*(i+1)],
                                   wrsb[:].rearrange("d (k e) -> d k e",
                                                     k=16)[:, k, :],
                                   start=(k == 0), stop=(k == 15))
                return ins
            sq_.add("tensor", f_rmm)
            sq_.add("vector", lambda v: v.reduce_max(s1[:, 0:1], ps8[:], AX.X))
            sq_.add("vector", lambda v: v.tensor_scalar(sx[:], ps8[:],
                s1[:, 0:1], None, ALU.subtract))
            sq_.add("scalar", lambda sc: sc.activation(sx[:], sx[:], AF.Exp))
            sq_.add("vector", lambda v: v.reduce_sum(s1[:, 1:2], sx[:], AX.X))
            sq_.add("vector", lambda v: v.reciprocal(s1[:, 1:2], s1[:, 1:2]))
            sq_.add("vector", lambda v: v.tensor_scalar(sx[:], sx[:],
                s1[:, 1:2], None, ALU.mult))
            sq_.add("vector", lambda v: v.reduce_max(s1[:, 0:1], sx[:], AX.X))
            sq_.add("vector", lambda v: v.tensor_scalar(row[:], sx[:],
                s1[:, 0:1], None, ALU.is_lt))
            sq_.add("vector", lambda v: v.tensor_mul(row[:], row[:], sx[:]))
            sq_.add("vector", lambda v: v.reduce_max(s1[:, 3:4], row[:], AX.X))
            sq_.add("vector", lambda v: v.tensor_scalar(row[:], sx[:],
                s1[:, 3:4], None, ALU.is_ge))
            sq_.add("vector", lambda v: v.tensor_mul(row[:], row[:], sx[:]))
            sq_.add("tensor", lambda t: t.transpose(pst8[:], row[:],
                                                    C["IDENT"][:]))
            sq_.add("scalar", lambda sc, i=i: sc.copy(wfT[:, 128*i:128*(i+1)],
                                                      pst8[:]))
        sq_.add("gpsimd", lambda g: g.dma_start(out=wr_dram, in_=wfT[:]),
                dma=True)
        sq_.emit()
    with (
        nc.sbuf_tensor(pre + "wg", [128, 2048], BF16) as wgt,
        nc.sbuf_tensor(pre + "wu", [128, 2048], BF16) as wut,
        nc.sbuf_tensor(pre + "hf", [128, 11 * 2048], BF16) as hf,
        nc.sbuf_tensor(pre + "wd", [128, 11 * 2048], BF16) as wd,
        nc.sbuf_tensor(pre + "sg", [128, 2048], BF16) as sgt,
        nc.sbuf_tensor(pre + "wrw", [1, 2048], F32) as wrow,
        nc.sbuf_tensor(pre + "wbc", [128, 2048], F32) as wbc,
        nc.sbuf_tensor(pre + "ev", [128, 2048], F32) as ev,
        nc.psum_tensor(pre + "pg", [128, 2048], F32) as pg,
        nc.psum_tensor(pre + "pu", [128, 2048], F32) as pu,
    ):
        po = pg
        NCH = T // 512
        for unit in range(10):
            is_sh = unit >= 8
            e = unit - 8 if is_sh else unit
            g4 = wsg4 if is_sh else wg4
            u4 = wsu4 if is_sh else wu4
            dT = wsdT if is_sh else wdTs
            sq_ = Seq(ctx, pre + "ex")
            if unit == 0:
                srt = ctx.sem(pre + "rt")
                def f_rtgate(sy, srt=srt):
                    sy.wait_ge(srt.h, srt.n)
                    return sy.nop()
                sq_.add("sync", f_rtgate)
            if not is_sh:
                sq_.add("sync", lambda sy, e=e: sy.dma_start(
                    out=wrow[:], in_=wr_dram[e:e+1, :]), dma=True)
                def f_wbc(t):
                    ins = None
                    for c in range(NCH):
                        ins = t.matmul(po[:, c*512:(c+1)*512], C["ONESR32"][:],
                                       wrow[0:1, c*512:(c+1)*512],
                                       start=True, stop=True)
                    return ins
                sq_.add("tensor", f_wbc)
                sq_.add("scalar", lambda sc: sc.copy(wbc[:], po[:]))
            for kf in range(11):
                sq_.add("gpsimd", lambda g, e=e, kf=kf, dT=dT: g.dma_start(
                    out=wd[:, kf*2048:(kf+1)*2048],
                    in_=dT[e, 128*kf:128*(kf+1), :]), dma=True)
            for ft in range(11):
                sq_.add("gpsimd", lambda g, e=e, ft=ft, g4=g4: g.dma_start(
                    out=wgt[:].rearrange("d (k o) -> d k o", k=16),
                    in_=g4[e, ft].rearrange("k d o -> d k o")), dma=True)
                sq_.add("gpsimd", lambda g, e=e, ft=ft, u4=u4: g.dma_start(
                    out=wut[:].rearrange("d (k o) -> d k o", k=16),
                    in_=u4[e, ft].rearrange("k d o -> d k o")), dma=True)
                def f_g(t):
                    ins = None
                    for k in range(16):
                        for c in range(NCH):
                            ins = t.matmul(pg[:, c*512:(c+1)*512],
                                           wgt[:, 128*k:128*(k+1)],
                                           xnT[:, k*T+c*512:k*T+(c+1)*512],
                                           start=(k == 0), stop=(k == 15))
                    return ins
                sq_.add("tensor", f_g)
                def f_u(t):
                    ins = None
                    for k in range(16):
                        for c in range(NCH):
                            ins = t.matmul(pu[:, c*512:(c+1)*512],
                                           wut[:, 128*k:128*(k+1)],
                                           xnT[:, k*T+c*512:k*T+(c+1)*512],
                                           start=(k == 0), stop=(k == 15))
                    return ins
                sq_.add("tensor", f_u)
                sq_.add("scalar", lambda sc: sc.activation(sgt[:], pg[:],
                                                           AF.Silu))
                sq_.add("vector", lambda v: v.tensor_mul(sgt[:], sgt[:],
                                                         pu[:]))
                if is_sh:
                    sq_.add("vector", lambda v, ft=ft: v.tensor_copy(
                        hf[:, ft*2048:(ft+1)*2048], sgt[:]))
                else:
                    sq_.add("vector", lambda v, ft=ft: v.tensor_mul(
                        hf[:, ft*2048:(ft+1)*2048], sgt[:], wbc[:]))
            for tt in range(T // 128):
                def f_dn(t, tt=tt):
                    ins = None
                    for kf in range(11):
                        for c in range(4):
                            ins = t.matmul(
                                po[:, c*512:(c+1)*512],
                                hf[:, kf*2048+128*tt:kf*2048+128*(tt+1)],
                                wd[:, kf*2048+c*512:kf*2048+(c+1)*512],
                                start=(kf == 0), stop=(kf == 10))
                    return ins
                sq_.add("tensor", f_dn)
                sq_.add("vector", lambda v: v.tensor_copy(ev[:], po[:]))
                if unit == 0:
                    sq_.add("gpsimd", lambda g, tt=tt: g.dma_start(
                        out=acc_dram[128*tt:128*(tt+1), :], in_=ev[:]),
                        dma=True)
                else:
                    sq_.add("gpsimd", lambda g, tt=tt: g.dma_start(
                        out=acc_dram[128*tt:128*(tt+1), :], in_=ev[:],
                        accum_op=ALU.add), dma=True)
            sq_.emit()
    with (
        nc.sbuf_tensor(pre + "xa", [128, D], F32) as xa,
        nc.sbuf_tensor(pre + "xc", [128, D], F32) as xc,
    ):
        sq_ = Seq(ctx, pre + "fin")
        sex = ctx.sem(pre + "ex")
        def f_gate(sy, sex=sex):
            sy.wait_ge(sex.h, sex.n)
            return sy.nop()
        sq_.add("sync", f_gate)
        for i in range(NT):
            sq_.add("sync", lambda sy, i=i: sy.dma_start(
                out=xa[:], in_=x_in_dram[128*i:128*(i+1), :]), dma=True)
            sq_.add("sync", lambda sy, i=i: sy.dma_start(
                out=xc[:], in_=acc_dram[128*i:128*(i+1), :]), dma=True)
            sq_.add("vector", lambda v: v.scalar_tensor_tensor(
                xa[:], xc[:], 0.5, xa[:], ALU.mult, ALU.add))
            sq_.add("gpsimd", lambda g, i=i: g.dma_start(
                out=x_out_dram[128*i:128*(i+1), :], in_=xa[:]), dma=True)
        sq_.emit()

def emit_rope_pass(ctx, qkvF_dram, bias_bc, cos_d, sin_d, qkv_out, T, C,
                   pre="rp"):
    nc = ctx.nc
    H16 = lambda ap: ap.rearrange("p (h c) -> p h c", h=16)
    with (
        nc.sbuf_tensor(pre + "x", [128, 6144], BF16) as xt,
        nc.sbuf_tensor(pre + "cs", [128, 256], F32) as cs,
        nc.sbuf_tensor(pre + "f", [128, 6144], F32) as xf,
        nc.sbuf_tensor(pre + "t1", [128, 2048], F32) as t1,
        nc.sbuf_tensor(pre + "t2", [128, 2048], F32) as t2,
        nc.sbuf_tensor(pre + "o", [128, 6144], BF16) as ob,
    ):
        sq_ = Seq(ctx, pre + "seq")
        for i in range(T // 128):
            sq_.add("sync", lambda sy, i=i: sy.dma_start(
                out=xt[:], in_=qkvF_dram[0:6144, 128*i:128*(i+1)],
                transpose=True), dma=True)
            sq_.add("sync", lambda sy, i=i: sy.dma_start(
                out=cs[:, 0:128], in_=cos_d[128*i:128*(i+1), :]), dma=True)
            sq_.add("sync", lambda sy, i=i: sy.dma_start(
                out=cs[:, 128:256], in_=sin_d[128*i:128*(i+1), :]), dma=True)
            sq_.add("vector", lambda v: v.tensor_add(xf[:], xt[:], bias_bc[:]))
            for qk in range(2):
                b0 = qk * 2048
                q_ = xf[:, b0:b0+2048]
                o_ = ob[:, b0:b0+2048]
                sq_.add("vector", lambda v, q_=q_: v.tensor_mul(
                    H16(t1[:]), H16(q_), bcast0(cs[:, 0:128], 16, 128)))
                sq_.add("vector", lambda v, q_=q_: v.tensor_mul(
                    H16(t2[:])[:, :, 0:64], H16(q_)[:, :, 64:128],
                    bcast0(cs[:, 128:256], 16, 128)[:, :, 0:64]))
                sq_.add("vector", lambda v, q_=q_: v.tensor_mul(
                    H16(t2[:])[:, :, 64:128], H16(q_)[:, :, 0:64],
                    bcast0(cs[:, 128:256], 16, 128)[:, :, 64:128]))
                sq_.add("vector", lambda v, o_=o_: v.tensor_sub(
                    H16(o_)[:, :, 0:64], H16(t1[:])[:, :, 0:64],
                    H16(t2[:])[:, :, 0:64]))
                sq_.add("vector", lambda v, o_=o_: v.tensor_add(
                    H16(o_)[:, :, 64:128], H16(t1[:])[:, :, 64:128],
                    H16(t2[:])[:, :, 64:128]))
            sq_.add("vector", lambda v: v.tensor_copy(ob[:, 4096:6144],
                                                      xf[:, 4096:6144]))
            sq_.add("gpsimd", lambda g, i=i: g.dma_start(
                out=qkv_out[128*i:128*(i+1), :], in_=ob[:]), dma=True)
        sq_.emit()

def rope_tables(positions):
    theta = 1.0 / 10000.0 ** (np.arange(0, 128, 2, dtype=np.float32) / 128.0)
    freqs = positions[:, None].astype(np.float32) * theta[None, :]
    freqs = np.concatenate([freqs, freqs], axis=-1)
    return np.cos(freqs).astype(np.float32), np.sin(freqs).astype(np.float32)

def prep_shared_L1(ins):
    d = {}
    d["wrT"] = np.ascontiguousarray(ins["L1_ffn_Wr"].T)
    wg4 = np.stack([np.ascontiguousarray(
        tile_wT(ins["L1_ffn_Wg"][e]).transpose(1, 0, 2, 3)) for e in range(8)])
    wu4 = np.stack([np.ascontiguousarray(
        tile_wT(ins["L1_ffn_Wu"][e]).transpose(1, 0, 2, 3)) for e in range(8)])
    d["wg4"], d["wu4"] = wg4, wu4
    d["wdTs"] = np.ascontiguousarray(
        np.stack([ins["L1_ffn_Wd"][e].T for e in range(8)]))
    sg4 = tile_wT(ins["L1_ffn_Wsg"]).transpose(1, 0, 2, 3)
    su4 = tile_wT(ins["L1_ffn_Wsu"]).transpose(1, 0, 2, 3)
    d["wsg4"] = np.ascontiguousarray(np.stack([sg4[0:11], sg4[11:22]]))
    d["wsu4"] = np.ascontiguousarray(np.stack([su4[0:11], su4[11:22]]))
    wsdT = ins["L1_ffn_Wsd"].T
    d["wsdT"] = np.ascontiguousarray(np.stack([wsdT[0:1408], wsdT[1408:2816]]))
    wqkv = np.concatenate([ins["L1_attn_wq"], ins["L1_attn_wk"],
                           ins["L1_attn_wv"]], axis=0)
    d["wqkv4"] = np.ascontiguousarray(tile_wT(wqkv).transpose(1, 0, 2, 3))
    bias = np.concatenate([ins["L1_attn_bq"], ins["L1_attn_bk"],
                           ins["L1_attn_bv"]])
    d["biasbc"] = np.tile(bias[None, :].astype(np.float32), (128, 1))
    d["lngbc"] = np.tile(ins["L1_attn_ln_g"][None, :].astype(np.float32),
                         (128, 1))
    d["lnbbc"] = np.tile(ins["L1_attn_ln_b"][None, :].astype(np.float32),
                         (128, 1))
    return d

def prep_launchC_inputs(ins, core, a_outs, b_outs, shared=None):
    b, p = core // 2, core % 2
    ce, co = 2 * b, 2 * b + 1
    sl = slice(2048 * p, 2048 * (p + 1))
    d = {}
    d["x1"] = a_outs[core]["x1"]
    d["pA"] = np.ascontiguousarray(b_outs[ce]["outp"][sl])
    d["pB"] = np.ascontiguousarray(b_outs[co]["outp"][sl])
    d["wqgu"] = a_outs[core]["wqgu"]
    d["wqp"] = a_outs[core]["wqp"]
    d.update(shared if shared is not None else prep_shared_L1(ins))
    cos, sin = rope_tables(np.arange(2048 * p, 2048 * (p + 1)))
    d["cosd"], d["sind"] = cos, sin
    d.update({"c" + k: np.asarray(v) for k, v in make_const_arrays().items()})
    return d

def build_launchC(T=2048, debug=False):
    import contextlib as _cl
    nc = bass.Bass("TRN2", target_bir_lowering=False)
    KIND = "ExternalOutput" if debug else "Internal"
    cd = {k: nc.dram_tensor("c" + k, CONST_SHAPE[k], CONST_DT[k],
                            kind="ExternalInput").ap() for k in CONST_DT}
    ein = lambda n, sh, dt=F32: nc.dram_tensor(n, sh, dt,
                                               kind="ExternalInput").ap()
    x1 = ein("x1", [T, D]); pA = ein("pA", [T, D]); pB = ein("pB", [T, D])
    wqgu = ein("wqgu", [128, 16, 128, 128], BF16)
    wqp = ein("wqp", [16, 64, 128, 128], BF16)
    wrT = ein("wrT", [2048, 8])
    wg4 = ein("wg4", [8, 11, 16, 128, 128]); wu4 = ein("wu4", [8, 11, 16, 128, 128])
    wdTs = ein("wdTs", [8, 1408, 2048])
    wsg4 = ein("wsg4", [2, 11, 16, 128, 128]); wsu4 = ein("wsu4", [2, 11, 16, 128, 128])
    wsdT = ein("wsdT", [2, 1408, 2048])
    wqkv4 = ein("wqkv4", [48, 16, 128, 128])
    biasbc = ein("biasbc", [128, 6144]); lngbc = ein("lngbc", [128, D])
    lnbbc = ein("lnbbc", [128, D])
    cosd = ein("cosd", [T, 128]); sind = ein("sind", [T, 128])
    x4 = nc.dram_tensor("x4", [T, D], F32, kind="ExternalOutput").ap()
    xn5 = nc.dram_tensor("xn5", [T, D], BF16, kind="ExternalOutput").ap()
    qkv = nc.dram_tensor("qkv", [T, 6144], BF16, kind="ExternalOutput").ap()
    x2 = nc.dram_tensor("x2_s", [T, D], F32, kind=KIND)
    x3 = nc.dram_tensor("x3_s", [T, D], F32, kind=KIND)
    xn3 = nc.dram_tensor("xn3_s", [T, D], BF16, kind=KIND)
    xn4 = nc.dram_tensor("xn4_s", [T, D], BF16, kind=KIND)
    xln = nc.dram_tensor("xln_s", [T, D], BF16, kind=KIND)
    p_scr = nc.dram_tensor("p_scr", [D, T], BF16)
    acc = nc.dram_tensor("acc_s", [T, D], F32, kind=KIND)
    wrd = nc.dram_tensor("wr_s", [8, T], F32, kind=KIND)
    qkvF = nc.dram_tensor("qkvF_s", [6144, T], BF16)
    with _cl.ExitStack() as stack:
        with nc.Block() as block:
            ctx = Ctx(nc, block, stack)
            C = load_consts(ctx, cd)
            emit_resid(ctx, [(x1, F32), (pA, F32), (pB, F32)], x2.ap(), T, C,
                       pre="r2")
            phase_barrier(ctx, semnames=["r2seq"])
            emit_ln(ctx, x2.ap(), xn3.ap(), T, C, pre="ln3")
            phase_barrier(ctx, rings=["ln3xo"])
            emit_ffn0(ctx, xn3.ap(), x2.ap(), x3.ap(), wqgu, wqp, p_scr.ap(),
                      T, C, pre="f3")
            phase_barrier(ctx, rings=["f3ro", "f3ps"])
            emit_ln(ctx, x3.ap(), xn4.ap(), T, C, pre="ln4")
            phase_barrier(ctx, rings=["ln4xo"])
            with nc.sbuf_tensor("xnT4", [128, 16 * T], BF16) as xnT:
                emit_xnT_load(ctx, xn4.ap(), xnT, T, pre="xt4")
                phase_barrier(ctx, semnames=["xt4seq"])
                emit_moe(ctx, xnT, x3.ap(), x4, wrT, wg4, wu4, wdTs, wsg4,
                         wsu4, wsdT, acc.ap(), wrd.ap(), T, C, pre="mo")
                phase_barrier(ctx, semnames=["moex", "mofin", "mort", "mosm"])
            with nc.sbuf_tensor("lng_s", [128, D], F32) as lng_sb, \
                 nc.sbuf_tensor("lnb_s", [128, D], F32) as lnb_sb:
                s0 = ctx.sem("consts")
                def f_ld(sy):
                    s0.inc(sy.dma_start(out=lng_sb[:], in_=lngbc))
                    s0.inc(sy.dma_start(out=lnb_sb[:], in_=lnbbc))
                    sy.wait_ge(s0.h, s0.n)
                block.sync(f_ld)
                emit_ln_double(ctx, x4, xn5, xln.ap(), lng_sb, lnb_sb, T, C,
                               pre="l5")
                phase_barrier(ctx, semnames=["l5seq"])
            emit_proj_feat(ctx, xln.ap(), wqkv4, qkvF.ap(), T, 48, C, pre="pC")
            phase_barrier(ctx, rings=["pCst"])
            with nc.sbuf_tensor("bia_s", [128, 6144], F32) as bias_sb:
                s1 = ctx.sem("consts")
                def f_ld2(sy):
                    s1.inc(sy.dma_start(out=bias_sb[:], in_=biasbc))
                    sy.wait_ge(s1.h, s1.n)
                block.sync(f_ld2)
                emit_rope_pass(ctx, qkvF.ap(), bias_sb, cosd, sind, qkv, T, C,
                               pre="rp")
            def fin(sy):
                s = ctx.sem("rpseq")
                sy.wait_ge(s.h, s.n)
            block.sync(fin)
    return nc

def prep_launchD_inputs(ins, core, c_outs):
    b, p = core // 2, core % 2
    ce, co = 2 * b, 2 * b + 1
    def getq(c):
        q = c_outs[c]["qkv"]
        return q.view(NPBF) if q.dtype == np.uint16 else q
    full = np.concatenate([getq(ce), getq(co)], axis=0)
    h0 = 8 * p
    d = {"qr": np.ascontiguousarray(full[:, 128*h0:128*(h0+8)]),
         "kr": np.ascontiguousarray(full[:, 2048+128*h0:2048+128*(h0+8)]),
         "vv": np.ascontiguousarray(full[:, 4096+128*h0:4096+128*(h0+8)]),
         "woT": np.ascontiguousarray(
             ins["L1_attn_wo"].T[1024*p:1024*(p+1), :])}
    d.update({"c" + k: np.asarray(v) for k, v in make_const_arrays().items()})
    mneg = np.triu(np.full((128, 128), -30000.0, dtype=np.float32), 1).astype(NPBF)
    d["mneg"] = mneg
    return d

def build_launchD():
    import contextlib as _cl
    S = S_FULL
    nc = bass.Bass("TRN2", target_bir_lowering=False)
    cd = {k: nc.dram_tensor("c" + k, CONST_SHAPE[k], CONST_DT[k],
                            kind="ExternalInput").ap() for k in CONST_DT}
    qr = nc.dram_tensor("qr", [S, 1024], BF16, kind="ExternalInput").ap()
    kr = nc.dram_tensor("kr", [S, 1024], BF16, kind="ExternalInput").ap()
    vv = nc.dram_tensor("vv", [S, 1024], BF16, kind="ExternalInput").ap()
    woT = nc.dram_tensor("woT", [1024, 2048], F32, kind="ExternalInput").ap()
    mneg = nc.dram_tensor("mneg", [128, 128], BF16, kind="ExternalInput").ap()
    outp = nc.dram_tensor("outp", [S, 2048], F32, kind="ExternalOutput").ap()
    p_scr = nc.dram_tensor("p_scr", [S, S], BF16)
    o_scr = nc.dram_tensor("o_scr", [1024, S], BF16)
    with _cl.ExitStack() as stack:
        with nc.Block() as block:
            ctx = Ctx(nc, block, stack)
            C = load_consts(ctx, cd)
            mn_sb = ctx.sb("mn", [128, 128], BF16)
            s0 = ctx.sem("consts")
            def f_ld(sy):
                s0.inc(sy.dma_start(out=mn_sb[:], in_=mneg))
                sy.wait_ge(s0.h, s0.n)
            block.sync(f_ld)
            SCL = 0.08838834764831845
            with (
                nc.sbuf_tensor("qT", [128, S], BF16) as qT,
                nc.sbuf_tensor("kT", [128, S], BF16) as kT,
                nc.sbuf_tensor("vh", [128, S], BF16) as vh,
                nc.sbuf_tensor("pt", [128, S], BF16) as pt,
                nc.sbuf_tensor("pb", [128, S], BF16) as pb,
                nc.sbuf_tensor("mx", [128, 4], F32) as mx,
                nc.sbuf_tensor("rs", [128, 32], F32) as rs,
                nc.sbuf_tensor("ob", [128, S], BF16) as ob,
                nc.psum_tensor("psS", [128, 4096], F32) as psS,
            ):
                for h in range(8):
                    sq_ = Seq(ctx, "dh")
                    sq_.add("sync", lambda sy, h=h: sy.dma_start(
                        out=qT[:], in_=qr[0:S, 128*h:128*(h+1)],
                        transpose=True), dma=True)
                    sq_.add("sync", lambda sy, h=h: sy.dma_start(
                        out=kT[:], in_=kr[0:S, 128*h:128*(h+1)],
                        transpose=True), dma=True)
                    sq_.add("sync", lambda sy, h=h: sy.dma_start(
                        out=vh[:].rearrange("p (j c) -> p j c", j=32),
                        in_=vv[:, 128*h:128*(h+1)].rearrange(
                            "(j p) c -> p j c", p=128)), dma=True)
                    for it in range(32):
                        w = 128 * (it + 1)
                        nch = (w + 511) // 512
                        def f_s(t, it=it, nch=nch, w=w):
                            ins = None
                            for c in range(nch):
                                ce_ = min(512 * (c + 1), w)
                                ins = t.matmul(psS[:, 512*c:ce_],
                                               qT[:, 128*it:128*(it+1)],
                                               kT[:, 512*c:ce_],
                                               start=True, stop=True)
                            return ins
                        sq_.add("tensor", f_s)
                        sq_.add("vector", lambda v, it=it: v.tensor_tensor(
                            psS[:, 128*it:128*(it+1)],
                            psS[:, 128*it:128*(it+1)], mn_sb[:], ALU.add))
                        sq_.add("vector", lambda v, w=w: v.reduce_max(
                            mx[:, 0:1], psS[:, 0:w], AX.X))
                        sq_.add("vector", lambda v: v.tensor_scalar(
                            mx[:, 1:2], mx[:, 0:1], -SCL, None, ALU.mult))
                        def f_exp(sc, it=it, w=w):
                            return sc.activation(pb[:, 0:w], psS[:, 0:w],
                                                 AF.Exp, bias=mx[:, 1:2],
                                                 scale=SCL,
                                                 accum_out=rs[:, it:it+1])
                        sq_.add("scalar", f_exp)
                        sq_.add("vector", lambda v, it=it: v.reciprocal(
                            rs[:, it:it+1], rs[:, it:it+1]))
                        sq_.add("vector", lambda v, it=it, w=w: v.tensor_mul(
                            pb[:, 0:w].unsqueeze(1),
                            pb[:, 0:w].unsqueeze(1),
                            rs[:, it:it+1].unsqueeze(2).broadcast_to(
                                [128, 1, w])))
                        sq_.add("gpsimd", lambda g, it=it, w=w, h=h:
                                g.dma_start(out=p_scr[128*it:128*(it+1), 0:w],
                                            in_=pb[:, 0:w]), dma=True)
                    for j in range(32):
                        sq_.add("sync", lambda sy, j=j: sy.dma_start(
                            out=pt[:], in_=p_scr[0:S, 128*j:128*(j+1)],
                            transpose=True), dma=True)
                        def f_av(t, j=j):
                            ins = None
                            for bk in range(8):
                                if j > 4 * bk + 3:
                                    continue
                                qs = max(512 * bk, 128 * j)
                                qe = 512 * (bk + 1)
                                ins = t.matmul(psS[:, qs:qe],
                                               vh[:, 128*j:128*(j+1)],
                                               pt[:, qs:qe],
                                               start=(j == 0),
                                               stop=(j == 4 * bk + 3))
                            return ins
                        sq_.add("tensor", f_av)
                    sq_.add("vector", lambda v: v.tensor_copy(ob[:], psS[:]))
                    sq_.add("gpsimd", lambda g, h=h: g.dma_start(
                        out=o_scr[128*h:128*(h+1), 0:S], in_=ob[:]),
                        dma=True)
                    sq_.emit()
            with (
                nc.sbuf_tensor("oT", [128, 8 * S], BF16) as oT,
                nc.sbuf_tensor("wo", [128, 8 * 2048], BF16) as wo,
                nc.sbuf_tensor("evd", [128, 2048], F32) as evd,
                nc.psum_tensor("psO", [128, 2048], F32) as psO,
            ):
                sq_ = Seq(ctx, "dw")
                for kch in range(8):
                    sq_.add("gpsimd", lambda g, kch=kch: g.dma_start(
                        out=wo[:, kch*2048:(kch+1)*2048],
                        in_=woT[128*kch:128*(kch+1), :]), dma=True)
                    sq_.add("sync", lambda sy, kch=kch: sy.dma_start(
                        out=oT[:, kch*S:(kch+1)*S],
                        in_=o_scr[128*kch:128*(kch+1), 0:S]), dma=True)
                for tt in range(32):
                    def f_mm(t, tt=tt):
                        ins = None
                        for kch in range(8):
                            for c in range(4):
                                ins = t.matmul(
                                    psO[:, c*512:(c+1)*512],
                                    oT[:, kch*S+128*tt:kch*S+128*(tt+1)],
                                    wo[:, kch*2048+c*512:kch*2048+(c+1)*512],
                                    start=(kch == 0), stop=(kch == 7))
                        return ins
                    sq_.add("tensor", f_mm)
                    sq_.add("vector", lambda v: v.tensor_copy(evd[:], psO[:]))
                    sq_.add("gpsimd", lambda g, tt=tt: g.dma_start(
                        out=outp[128*tt:128*(tt+1), :], in_=evd[:]), dma=True)
                sq_.emit()
            def fin(sy):
                s = ctx.sem("dw")
                sy.wait_ge(s.h, s.n)
            block.sync(fin)
    return nc

def prep_launchE_inputs(ins, core, c_outs, d_outs, shared=None):
    b, p = core // 2, core % 2
    sl = slice(2048 * p, 2048 * (p + 1))
    if shared is None:
        shared = prep_shared_L1(ins)
    d = {}
    d["x4"] = c_outs[core]["x4"]
    d["xn5"] = c_outs[core]["xn5"]
    d["pA"] = np.ascontiguousarray(d_outs[2*b]["outp"][sl])
    d["pB"] = np.ascontiguousarray(d_outs[2*b+1]["outp"][sl])
    d["bobc"] = np.tile(ins["L1_attn_bo"][None, :].astype(np.float32), (128, 1))
    for k in ["wrT", "wg4", "wu4", "wdTs", "wsg4", "wsu4", "wsdT"]:
        d[k] = shared[k]
    d.update({"c" + k: np.asarray(v) for k, v in make_const_arrays().items()})
    return d

def build_launchE(T=2048):
    import contextlib as _cl
    KIND = "Internal"
    nc = bass.Bass("TRN2", target_bir_lowering=False)
    cd = {k: nc.dram_tensor("c" + k, CONST_SHAPE[k], CONST_DT[k],
                            kind="ExternalInput").ap() for k in CONST_DT}
    ein = lambda n, sh, dt=F32: nc.dram_tensor(n, sh, dt,
                                               kind="ExternalInput").ap()
    x4 = ein("x4", [T, D])
    xn5 = ein("xn5", [T, D], BF16)
    pA = ein("pA", [T, D]); pB = ein("pB", [T, D])
    bobc = ein("bobc", [128, D])
    wrT = ein("wrT", [2048, 8])
    wg4 = ein("wg4", [8, 11, 16, 128, 128]); wu4 = ein("wu4", [8, 11, 16, 128, 128])
    wdTs = ein("wdTs", [8, 1408, 2048])
    wsg4 = ein("wsg4", [2, 11, 16, 128, 128]); wsu4 = ein("wsu4", [2, 11, 16, 128, 128])
    wsdT = ein("wsdT", [2, 1408, 2048])
    x6 = nc.dram_tensor("x6", [T, D], F32, kind="ExternalOutput").ap()
    x5 = nc.dram_tensor("x5_s", [T, D], F32)
    xn6 = nc.dram_tensor("xn6_s", [T, D], BF16)
    acc = nc.dram_tensor("acc_s", [T, D], F32, kind=KIND)
    wrd = nc.dram_tensor("wr_s", [8, T], F32, kind=KIND)
    with _cl.ExitStack() as stack:
        with nc.Block() as block:
            ctx = Ctx(nc, block, stack)
            C = load_consts(ctx, cd)
            bo_sb = ctx.sb("bo", [128, D], F32)
            xn5f_sb = ctx.sb("x5f", [128, D], F32)
            s0 = ctx.sem("consts")
            def f_ld(sy):
                s0.inc(sy.dma_start(out=bo_sb[:], in_=bobc))
                sy.wait_ge(s0.h, s0.n)
            block.sync(f_ld)
            with nc.sbuf_tensor("racc", [128, D], F32) as racc, \
                 nc.sbuf_tensor("rtb", [128, D], F32) as rtb, \
                 nc.sbuf_tensor("rt16", [128, D], BF16) as rt16:
                sq_ = Seq(ctx, "r5")
                for i in range(T // 128):
                    sq_.add("sync", lambda sy, i=i: sy.dma_start(
                        out=racc[:], in_=x4[128*i:128*(i+1), :]), dma=True)
                    for dram in [pA, pB]:
                        sq_.add("sync", lambda sy, i=i, dram=dram: sy.dma_start(
                            out=rtb[:], in_=dram[128*i:128*(i+1), :]), dma=True)
                        sq_.add("vector", lambda v: v.tensor_add(racc[:],
                                                                 racc[:],
                                                                 rtb[:]))
                    sq_.add("sync", lambda sy, i=i: sy.dma_start(
                        out=rt16[:], in_=xn5[128*i:128*(i+1), :]), dma=True)
                    sq_.add("vector", lambda v: v.tensor_add(racc[:], racc[:],
                                                             rt16[:]))
                    sq_.add("vector", lambda v: v.tensor_add(racc[:], racc[:],
                                                             bo_sb[:]))
                    sq_.add("gpsimd", lambda g, i=i: g.dma_start(
                        out=x5.ap()[128*i:128*(i+1), :], in_=racc[:]), dma=True)
                sq_.emit()
            phase_barrier(ctx, semnames=["r5"])
            emit_ln(ctx, x5.ap(), xn6.ap(), T, C, pre="ln6")
            phase_barrier(ctx, rings=["ln6xo"])
            xnT = ctx.sb("xnT6", [128, 16 * T], BF16)
            emit_xnT_load(ctx, xn6.ap(), xnT, T, pre="xt6")
            phase_barrier(ctx, semnames=["xt6seq"])
            emit_moe(ctx, xnT, x5.ap(), x6, wrT, wg4, wu4, wdTs, wsg4, wsu4,
                     wsdT, acc.ap(), wrd.ap(), T, C, pre="mo")
            def fin(sy):
                s = ctx.sem("mofin")
                sy.wait_ge(s.h, s.n)
            block.sync(fin)
    return nc
_CACHE = {}

def _bld(name, fn):
    if name not in _CACHE:
        _CACHE[name] = fn()
    return _CACHE[name]

def _cv(d):
    return {k: (v.view(NPBF) if v.dtype == np.uint16 else v)
            for k, v in d.items()}

def kernel(**ins):
    cores = list(range(8))
    shA = prep_shared_A(ins)
    ncA = _bld("A", lambda: build_launchA(2048))
    ra = run_bass_kernel_spmd(ncA, [prep_launchA_inputs(ins, c, shA)
                                    for c in cores], core_ids=cores)
    a_outs = {c: _cv(ra.results[c]) for c in cores}
    ncB = _bld("B", lambda: build_launchB())
    rb = run_bass_kernel_spmd(ncB, [prep_launchB_inputs(ins, c, a_outs)
                                    for c in cores], core_ids=cores)
    b_outs = {c: _cv(rb.results[c]) for c in cores}
    sh1 = prep_shared_L1(ins)
    ncC = _bld("C", lambda: build_launchC(2048))
    rc = run_bass_kernel_spmd(ncC, [prep_launchC_inputs(ins, c, a_outs,
                                                        b_outs, sh1)
                                    for c in cores], core_ids=cores)
    c_outs = {c: _cv(rc.results[c]) for c in cores}
    ncD = _bld("D", lambda: build_launchD())
    rd = run_bass_kernel_spmd(ncD, [prep_launchD_inputs(ins, c, c_outs)
                                    for c in cores], core_ids=cores)
    d_outs = {c: _cv(rd.results[c]) for c in cores}
    ncE = _bld("E", lambda: build_launchE(2048))
    re_ = run_bass_kernel_spmd(ncE, [prep_launchE_inputs(ins, c, c_outs,
                                                         d_outs, sh1)
                                     for c in cores], core_ids=cores)
    out = np.empty((4, 4096, 2048), dtype=np.float32)
    for c in cores:
        b, p = c // 2, c % 2
        out[b, 2048*p:2048*(p+1)] = re_.results[c]["x6"]
    return out
